# version 12
# baseline (speedup 1.0000x reference)
import contextlib
import numpy as np
import concourse.bass as bass
import concourse.mybir as mybir
from concourse.bass_utils import run_bass_kernel_spmd

F32 = mybir.dt.float32
BF16 = mybir.dt.bfloat16
I8 = mybir.dt.int8
I32 = mybir.dt.int32
AF = mybir.ActivationFunctionType
ALU = mybir.AluOpType
AX = mybir.AxisListType

TPC = 2048
NT = 16
D = 1024
ND = 8
NI = 4096
NHC = 32
NOB = 8
HEADS = 8
EPS = 1e-5
NW = float(NI * D)

ENGINES = ("S", "G", "T", "V", "A")
SEMS = ("dS", "dSl0", "dSl1", "dH0", "dH1", "dG", "dC0", "dC1", "dGh0", "dGh1",
        "P", "V", "A", "G", "CC", "bS", "bG", "bT", "bV", "bA")

OVB = 102400


def build_nc(phases=12, gu_limit=99, gu_mode=6, dbg=False, q_serial=False):
    nc = bass.Bass("TRN2", target_bir_lowering=False, debug=False,
                   detect_race_conditions=False)

    x_in = nc.dram_tensor("x_in", [TPC, D], F32, kind="ExternalInput").ap()
    wkvT = nc.dram_tensor("wkvT", [D, 2048], F32, kind="ExternalInput").ap()
    wqT_d = nc.dram_tensor("wqT", [D, 1024], F32, kind="ExternalInput").ap()
    woT_d = nc.dram_tensor("woT", [D, 1024], F32, kind="ExternalInput").ap()
    wgT = nc.dram_tensor("wgT", [NOB, 128, 4096], F32, kind="ExternalInput").ap()
    wuT = nc.dram_tensor("wuT", [NOB, 128, 4096], F32, kind="ExternalInput").ap()
    wdT = nc.dram_tensor("wdT", [NI, D], F32, kind="ExternalInput").ap()
    bkvB = nc.dram_tensor("bkvB", [128, 2048], F32, kind="ExternalInput").ap()
    bqB = nc.dram_tensor("bqB", [128, 1024], F32, kind="ExternalInput").ap()
    boB = nc.dram_tensor("boB", [128, 1024], F32, kind="ExternalInput").ap()
    idf_in = nc.dram_tensor("idf_in", [128, 128], F32, kind="ExternalInput").ap()
    ones_in = nc.dram_tensor("ones_in", [128, 128], F32, kind="ExternalInput").ap()
    eps_in = nc.dram_tensor("eps_in", [128, 1], F32, kind="ExternalInput").ap()
    y_out = nc.dram_tensor("y_out", [TPC, D], F32, kind="ExternalOutput").ap()
    if dbg:
        o_tg = nc.dram_tensor("o_tg", [128, 4096], BF16, kind="ExternalOutput").ap()
        o_h = nc.dram_tensor("o_h", [128, 2048], F32, kind="ExternalOutput").ap()
        o_xq = nc.dram_tensor("o_xq", [128, 2048], BF16, kind="ExternalOutput").ap()
        o_mb = nc.dram_tensor("o_mb", [128, 12], F32, kind="ExternalOutput").ap()
        o_st = nc.dram_tensor("o_st", [128, 64], F32, kind="ExternalOutput").ap()
        o_sx = nc.dram_tensor("o_sx", [128, 2048], F32, kind="ExternalOutput").ap()
        o_qr = nc.dram_tensor("o_qr", [1, 2048], F32, kind="ExternalOutput").ap()

    h_dram = nc.dram_tensor("h_dram", [NHC, 128, TPC], F32).ap()
    tern_g = nc.dram_tensor("tern_g", [NOB, 128, 4096], BF16).ap()
    tern_u = nc.dram_tensor("tern_u", [NOB, 128, 4096], BF16).ap()
    tern_d = nc.dram_tensor("tern_d", [NHC, 128, 1024], BF16).ap()
    cc_in = nc.dram_tensor("cc_in", [128, 1024], F32).ap()
    cc_out = nc.dram_tensor("cc_out", [128, 1024], F32).ap()

    x_sb = nc.alloc_sbuf_tensor("x_sb", [128, NT * 1024], F32)
    idf = nc.alloc_sbuf_tensor("idf", [128, 128], F32)
    onesf = nc.alloc_sbuf_tensor("onesf", [128, 128], F32)
    epsc = nc.alloc_sbuf_tensor("epsc", [128, 1], F32)
    csb = nc.alloc_sbuf_tensor("csb", [128, 1024], F32)
    cpart = nc.alloc_sbuf_tensor("cpart", [128, 1024], F32)
    st_names = ["macc2", "vacc", "mu_neg", "inv", "rmx", "rmn", "c1n", "cB",
                "qv", "maxh", "cdown", "sqh", "t16a", "t16b"]
    st = {nm: nc.alloc_sbuf_tensor("st_" + nm, [128, 16], F32) for nm in st_names}
    macc = nc.alloc_sbuf_tensor("macc", [128, 64], F32)
    mrow = nc.alloc_sbuf_tensor("mrow", [128, 4], F32)
    msc2 = nc.alloc_sbuf_tensor("msc2", [1, 4], F32)
    mB = nc.alloc_sbuf_tensor("mB", [128, 4], F32)
    mBh = nc.alloc_sbuf_tensor("mBh", [128, 4], F32)
    mBhn = nc.alloc_sbuf_tensor("mBhn", [128, 4], F32)

    def ov(name, shape, dtype, off):
        assert off >= 0
        return nc.alloc_sbuf_tensor_at(name, shape, dtype, offset=OVB + off)

    xqT = ov("xqT", [128, ND * TPC], BF16, 0)
    uq = [ov(f"uq{i}", [128, 1024], F32, 32768 + i * 4096) for i in range(2)]
    kfb = [ov(f"kfb{i}", [128, 1024], F32, 40960 + i * 4096) for i in range(2)]
    kqf = [ov(f"kqf{i}", [128, 1024], F32, 49152 + i * 4096) for i in range(2)]
    vres = [ov(f"vres{i}", [128, 1024], F32, 57344 + i * 4096) for i in range(2)]
    k8b = [ov(f"k8b{i}", [128, 1024], I8, 65536 + i * 1024) for i in range(2)]
    scr = ov("scr", [128, 2048], F32, 67584)
    xqi = ov("xqi", [128, 1024], I32, 67584)
    stag = [ov(f"stag{i}", [128, 2048], F32, 75776 + i * 8192) for i in range(2)]
    terng_sb = ov("terng_sb", [128, 4096], BF16, 92160)
    ternu_sb = ov("ternu_sb", [128, 4096], BF16, 100352)
    terng_sb2 = ov("terng_sb2", [128, 4096], BF16, 53248)
    ternu_sb2 = nc.alloc_sbuf_tensor_at("ternu_sb2", [128, 4096], BF16, offset=94208)
    sgb = ov("sgb", [128, 1024], F32, 61440)
    g2b = ov("g2b", [128, 1024], F32, 108544)
    u2b = ov("u2b", [128, 1024], F32, 112640)
    hbb = [ov(f"hbb{i}", [128, 1024], F32, 116736 + i * 4096) for i in range(2)]
    habs = ov("habs", [128, 1024], F32, 32768)
    sxB = ov("sxB", [128, 2048], F32, 36864)
    mxt = ov("mxt", [128, 2048], F32, 45056)
    qrow = ov("qrow", [1, 2048], F32, 53248)
    terndn = ov("terndn", [128, NHC * 1024], BF16, 0)
    wdstag = [ov(f"wdstag{i}", [128, 1024], F32, 65536 + i * 4096) for i in range(2)]
    wdscr = ov("wdscr", [128, 1024], F32, 73728)
    hq = ov("hq", [128, NHC * 512], BF16, 65536)
    hrb = [ov(f"hrb{i}", [128, 512], F32, 98304 + i * 2048) for i in range(2)]
    uq5 = [ov(f"uq5{i}", [128, 512], F32, 102400 + i * 2048) for i in range(2)]
    kf5 = [ov(f"kf5{i}", [128, 512], F32, 106496 + i * 2048) for i in range(2)]
    k85 = [ov(f"k85{i}", [128, 512], I8, 110592 + i * 512) for i in range(2)]
    v5 = [ov(f"v5{i}", [128, 512], F32, 111616 + i * 2048) for i in range(2)]
    t15 = [ov(f"t15{i}", [128, 512], F32, 115712 + i * 2048) for i in range(2)]
    sBq = ov("sBq", [128, 512], F32, 119808)
    srow = ov("srow", [1, 512], F32, 121856)
    k5i = ov("k5i", [128, 512], I32, 123904)
    kvslab = ov("kvslab", [128, ND * 2048], F32, 0)
    kvt = [ov(f"kvt{i}", [128, 2048], F32, 65536 + i * 8192) for i in range(2)]
    xlb = ov("xlb", [128, 1024], F32, 81920)
    xlT = [ov(f"xlT{i}", [128, 1024], F32, 86016 + i * 4096) for i in range(2)]
    bkv_sb = ov("bkv_sb", [128, 2048], F32, 94208)
    wq_sb = ov("wq_sb", [128, ND * 1024], F32, 0)
    wo_sb = ov("wo_sb", [128, ND * 1024], F32, 32768)
    qtb = [ov(f"qtb{i}", [128, 1024], F32, 65536 + i * 4096) for i in range(2)]
    qT_sb = ov("qT_sb", [128, 1024], F32, 73728)
    qT_sb2 = ov("qT_sb2", [128, 1024], F32, 106496)
    attT = [ov(f"attT{i}", [128, 1024], F32, 77824 + i * 4096) for i in range(2)]
    xlb7 = ov("xlb7", [128, 1024], F32, 86016)
    xlT7 = [ov(f"xlT7{i}", [128, 1024], F32, 90112 + i * 4096) for i in range(2)]
    bq_sb = ov("bq_sb", [128, 1024], F32, 98304)
    bo_sb = ov("bo_sb", [128, 1024], F32, 102400)

    pb = [nc.alloc_psum_tensor(f"pb{i}", [128, 512], F32) for i in range(8)]

    x_in3 = x_in.rearrange("(t p) d -> p t d", p=128)
    y3 = y_out.rearrange("(t p) d -> p t d", p=128)
    wdT3 = wdT.rearrange("(c p) o -> c p o", p=128)
    wkvT4 = wkvT.rearrange("(c p) o -> p c o", p=128)
    wqT4 = wqT_d.rearrange("(c p) o -> p c o", p=128)
    woT4 = woT_d.rearrange("(c p) o -> p c o", p=128)
    tern_d3 = tern_d.rearrange("c p o -> p c o")

    with contextlib.ExitStack() as stack:
        SEMH = {s: stack.enter_context(nc.semaphore(s)) for s in SEMS}
        block = stack.enter_context(nc.Block())

        def script(role, E):
            n = {s: 0 for s in SEMS}
            evac_cnt = {i: 0 for i in range(8)}

            def bump(sem, k, inst=None):
                n[sem] += k
                if inst is not None:
                    inst.then_inc(SEMH[sem], k)

            def wait(eng, sem, val=None):
                if role == eng:
                    E.wait_ge(SEMH[sem], n[sem] if val is None else val)

            def dma(eng, sem, out, in_):
                if role == eng:
                    bump(sem, 16, E.dma_start(out=out, in_=in_))
                else:
                    bump(sem, 16)

            def op(eng, fn, tick=None, drain=False):
                if role == eng:
                    inst = fn(E)
                    if tick is not None:
                        bump(tick, 1, inst)
                    if drain:
                        E.drain()
                elif tick is not None:
                    bump(tick, 1)

            def tick(eng, sem):
                if role == eng:
                    bump(sem, 1, E.nop())
                else:
                    bump(sem, 1)

            def barrier():
                if role == "S":
                    for s in ("dS", "dSl0", "dSl1", "dH0", "dH1"):
                        E.wait_ge(SEMH[s], n[s])
                if role == "G":
                    for s in ("dG", "dC0", "dC1", "dGh0", "dGh1", "CC"):
                        E.wait_ge(SEMH[s], n[s])
                own = {"T": "P", "V": "V", "A": "A", "G": "G"}
                if role in own:
                    E.wait_ge(SEMH[own[role]], n[own[role]])
                bar = {"S": "bS", "G": "bG", "T": "bT", "V": "bV", "A": "bA"}
                for eng in ENGINES:
                    if role == eng:
                        bump(bar[eng], 1, E.nop())
                    else:
                        bump(bar[eng], 1)
                for eng in ENGINES:
                    if eng != role:
                        E.wait_ge(SEMH[bar[eng]], n[bar[eng]])

            x_sb3 = x_sb[:, :].rearrange("p (t d) -> p t d", d=1024)
            for q4 in range(4):
                dma("S", "dS", x_sb3[:, q4 * 4:(q4 + 1) * 4, :],
                    x_in3[:, q4 * 4:(q4 + 1) * 4, :])
            dma("S", "dS", idf[:], idf_in)
            dma("S", "dS", onesf[:], ones_in)
            dma("S", "dS", epsc[:], eps_in)

            scan_a = {0: 0, 1: 0}
            pieces = []
            for c in range(NOB):
                pieces.append((wgT, c, 0, 2048, 0 + 2 * c))
                pieces.append((wgT, c, 2048, 2048, 1 + 2 * c))
            for c in range(NOB):
                pieces.append((wuT, c, 0, 2048, 16 + 2 * c))
                pieces.append((wuT, c, 2048, 2048, 17 + 2 * c))
            for c in range(NHC):
                pieces.append((wdT3, c, 0, 1024, 32 + c))
            scan_v = 0
            for ci, (src, c, o0, w, col) in enumerate(pieces):
                sl = ci % 2
                dsem = "dSl0" if sl == 0 else "dSl1"
                if sl == 0:
                    wait("S", "A", scan_a[0])
                else:
                    wait("S", "V", scan_v)
                dma("S", dsem, stag[sl][:, 0:w], src[c][:, o0:o0 + w])
                if sl == 0:
                    wait("A", dsem)
                    op("A", lambda e, sl=sl, w=w, col=col: e.activation(
                        out=scr[:, 0:w], in_=stag[sl][:, 0:w], func=AF.Abs, bias=0.0,
                        scale=1.0, accum_out=macc[:, col:col + 1]), tick="A", drain=True)
                    scan_a[0] = n["A"]
                else:
                    wait("V", dsem)
                    op("V", lambda e, sl=sl, w=w: e.tensor_single_scalar(
                        sxB[:, 0:w], stag[sl][:, 0:w], -1.0, ALU.mult))
                    op("V", lambda e, sl=sl, w=w: e.tensor_tensor(
                        sxB[:, 0:w], sxB[:, 0:w], stag[sl][:, 0:w], ALU.max))
                    op("V", lambda e, w=w, col=col: e.tensor_reduce(
                        macc[:, col:col + 1], sxB[:, 0:w], AX.X, ALU.add), tick="V", drain=True)
                    scan_v = n["V"]
            wait("V", "A")
            op("V", lambda e: e.tensor_reduce(mrow[:, 0:1], macc[:, 0:16], AX.X, ALU.add))
            op("V", lambda e: e.tensor_reduce(mrow[:, 1:2], macc[:, 16:32], AX.X, ALU.add))
            op("V", lambda e: e.tensor_reduce(mrow[:, 2:3], macc[:, 32:64], AX.X, ALU.add), tick="V", drain=True)
            wait("T", "dS", 112)
            wait("T", "V")
            op("T", lambda e: e.matmul(pb[0][0:1, 0:3], onesf[:, 0:1], mrow[:, 0:3],
                                       start=True, stop=True), tick="P")
            wait("V", "P")
            op("V", lambda e: e.tensor_scalar(msc2[0:1, 0:3], pb[0][0:1, 0:3],
                                              1.0 / NW, EPS, ALU.mult, ALU.max), tick="V", drain=True)
            wait("T", "V")
            op("T", lambda e: e.matmul(pb[1][:, 0:3], onesf[0:1, :], msc2[0:1, 0:3],
                                       start=True, stop=True), tick="P")
            wait("V", "P")
            op("V", lambda e: e.tensor_copy(mB[:, 0:3], pb[1][:, 0:3]), drain=True)
            op("V", lambda e: e.tensor_single_scalar(mBhn[:, 0:3], mB[:, 0:3], -0.5, ALU.mult), drain=True)
            op("V", lambda e: e.tensor_single_scalar(mBh[:, 0:3], mB[:, 0:3], 0.5, ALU.mult), tick="V", drain=True)
            barrier()

            def phase_quant_x():
                wait("A", "dS", 64)
                for tt in range(NT):
                    xt = x_sb[:, tt * 1024:(tt + 1) * 1024]
                    op("A", lambda e, xt=xt, tt=tt: e.activation(
                        out=scr[:, 0:1024], in_=xt, func=AF.Identity, bias=0.0,
                        scale=1.0, accum_out=st["macc2"][:, tt:tt + 1]), drain=True)
                tick("A", "A")
                wait("V", "A")
                op("V", lambda e: e.tensor_single_scalar(st["mu_neg"][:], st["macc2"][:], -1.0 / D, ALU.mult), tick="V", drain=True)
                wait("A", "V")
                for tt in range(NT):
                    xt = x_sb[:, tt * 1024:(tt + 1) * 1024]
                    op("A", lambda e, xt=xt, tt=tt: e.activation(
                        out=scr[:, 0:1024], in_=xt, func=AF.Square,
                        bias=st["mu_neg"][:, tt:tt + 1], scale=1.0,
                        accum_out=st["vacc"][:, tt:tt + 1]), drain=True)
                tick("A", "A")
                wait("V", "A")
                op("V", lambda e: e.tensor_single_scalar(st["t16a"][:], st["vacc"][:], 1.0 / D, ALU.mult), tick="V", drain=True)
                wait("A", "V")
                op("A", lambda e: e.activation(out=st["t16b"][:], in_=st["t16a"][:],
                                               func=AF.Sqrt, bias=epsc[:], scale=1.0), tick="A", drain=True)
                wait("V", "A")
                op("V", lambda e: e.reciprocal(st["inv"][:], st["t16b"][:]), drain=True)
                for tt in range(NT):
                    xt = x_sb[:, tt * 1024:(tt + 1) * 1024]
                    op("V", lambda e, xt=xt, tt=tt: e.tensor_reduce(st["rmx"][:, tt:tt + 1], xt, AX.X, ALU.max))
                    op("V", lambda e, xt=xt, tt=tt: e.tensor_reduce(st["rmn"][:, tt:tt + 1], xt, AX.X, ALU.min),
                       drain=(tt == NT - 1))
                op("V", lambda e: e.tensor_tensor(st["t16a"][:], st["rmx"][:], st["mu_neg"][:], ALU.add), drain=True)
                op("V", lambda e: e.tensor_tensor(st["t16b"][:], st["rmn"][:], st["mu_neg"][:], ALU.add), drain=True)
                op("V", lambda e: e.scalar_tensor_tensor(st["t16a"][:], st["t16b"][:], -1.0,
                                                         st["t16a"][:], ALU.mult, ALU.max), drain=True)
                op("V", lambda e: e.tensor_tensor(st["t16a"][:], st["t16a"][:], st["inv"][:], ALU.mult), drain=True)
                op("V", lambda e: e.tensor_scalar_max(st["t16a"][:], st["t16a"][:], EPS), drain=True)
                op("V", lambda e: e.tensor_single_scalar(st["qv"][:], st["t16a"][:], 1.0 / 127.0, ALU.mult), drain=True)
                op("V", lambda e: e.reciprocal(st["t16b"][:], st["t16a"][:]), drain=True)
                op("V", lambda e: e.tensor_tensor(st["t16b"][:], st["t16b"][:], st["inv"][:], ALU.mult), drain=True)
                op("V", lambda e: e.tensor_single_scalar(st["c1n"][:], st["t16b"][:], -127.0, ALU.mult), drain=True)
                op("V", lambda e: e.tensor_tensor(st["cB"][:], st["mu_neg"][:], st["c1n"][:], ALU.mult), tick="V", drain=True)
                stats_v = n["V"]
                wait("A", "V", stats_v)
                vq_done = {0: 0, 1: 0}
                tq_done = {0: 0, 1: 0}
                for tt in range(NT):
                    sl = tt % 2
                    if tt >= 2:
                        wait("A", "V", vq_done[sl])
                    xt = x_sb[:, tt * 1024:(tt + 1) * 1024]
                    op("A", lambda e, xt=xt, tt=tt, sl=sl: e.activation(
                        out=uq[sl][:], in_=xt, func=AF.Identity,
                        bias=st["cB"][:, tt:tt + 1], scale=st["c1n"][:, tt:tt + 1]), tick="A")
                    wait("V", "A")
                    wait("V", "P", tq_done[sl])
                    op("V", lambda e, sl=sl: e.tensor_copy(xqi[:], uq[sl][:]))
                    op("V", lambda e, sl=sl: e.tensor_copy(kfb[sl][:], xqi[:]))
                    op("V", lambda e, sl=sl: e.tensor_tensor(vres[sl][:], uq[sl][:], kfb[sl][:], ALU.subtract))
                    op("V", lambda e, sl=sl: e.scalar_tensor_tensor(kqf[sl][:], vres[sl][:], 0.5,
                                                                    kfb[sl][:], ALU.is_gt, ALU.add))
                    op("V", lambda e, sl=sl: e.scalar_tensor_tensor(kqf[sl][:], vres[sl][:], -0.5,
                                                                    kqf[sl][:], ALU.is_lt, ALU.subtract), tick="V")
                    vq_done[sl] = n["V"]
                    wait("T", "V", vq_done[sl])
                    for dc in range(ND):
                        tb = 6 + (dc % 2)
                        wait("T", "V", evac_cnt[tb])
                        op("T", lambda e, dc=dc, sl=sl, tb=tb: e.transpose(
                            pb[tb][:, 0:128], kqf[sl][:, dc * 128:(dc + 1) * 128], idf[:]), tick="P")
                        wait("V", "P")
                        op("V", lambda e, dc=dc, tt=tt, tb=tb: e.tensor_copy(
                            xqT[:, dc * TPC + tt * 128: dc * TPC + (tt + 1) * 128],
                            pb[tb][:, 0:128]), tick="V")
                        evac_cnt[tb] = n["V"]
                    tq_done[sl] = n["P"]
                    if q_serial:
                        barrier()

            def phase_gateup(call, gu_limit=gu_limit, gu_mode=gu_mode):
                op("G", lambda e: e.memset(mxt[:], 0.0), tick="G")
                mxt_zeroed = n["G"]
                wait("V", "G", mxt_zeroed)
                for tt in range(NT):
                    tb = 6 + (tt % 2)
                    wait("T", "V", evac_cnt[tb])
                    op("T", lambda e, tt=tt, tb=tb: e.transpose(
                        pb[tb][0:1, 0:128], st["qv"][:, tt:tt + 1], idf[:]), tick="P")
                    wait("V", "P")
                    op("V", lambda e, tt=tt, tb=tb: e.tensor_copy(
                        qrow[0:1, tt * 128:(tt + 1) * 128], pb[tb][0:1, 0:128]), tick="V")
                    evac_cnt[tb] = n["V"]
                for b in range(4):
                    tb = 6 + (b % 2)
                    wait("T", "V", evac_cnt[tb])
                    op("T", lambda e, b=b, tb=tb: e.matmul(
                        pb[tb][:, 0:512], onesf[0:1, :], qrow[0:1, b * 512:(b + 1) * 512],
                        start=True, stop=True), tick="P")
                    wait("V", "P")
                    op("V", lambda e, b=b, tb=tb: e.tensor_copy(
                        sxB[:, b * 512:(b + 1) * 512], pb[tb][:, 0:512]), tick="V")
                    evac_cnt[tb] = n["V"]
                qrow_cons = n["P"]
                if gu_limit == 0:
                    return
                stag3 = [s[:, :].rearrange("p (c o) -> p c o", o=512) for s in stag]
                tern_sbs = [[terng_sb, terng_sb2], [ternu_sb, ternu_sb2]]
                gu_slabdone = {}
                pool_tern = {}
                slab_ld = {}
                stag_free = {0: 0, 1: 0}
                evacV = {0: 0, 1: 0}
                habs_done = {0: 0, 1: 0}
                hstore = {0: 0, 1: 0}
                last_hmul_G = 0
                dg_store = {}
                nob = min(NOB, gu_limit)

                def emit_slab(ob):
                    for mi in range(2):
                        wsrc = [wgT, wuT][mi]
                        tdram = [tern_g, tern_u][mi]
                        tsb = tern_sbs[mi][ob % 2]
                        dsem = "dSl0" if mi == 0 else "dSl1"
                        if call == 0:
                            for half in range(2):
                                sl = mi
                                wait("S", "V", stag_free[sl])
                                dma("S", dsem, stag[sl][:],
                                    wsrc[ob][:, half * 2048:(half + 1) * 2048])
                                wait("V", dsem)
                                if half == 0:
                                    if ob == 1 and mi == 0:
                                        wait("V", "P", qrow_cons)
                                    if ob >= 2:
                                        wait("V", "P", gu_slabdone[(mi, ob - 2)])
                                        wait("V", "dG", dg_store[(mi, ob - 2)])
                                op("V", lambda e, sl=sl, mi=mi: e.tensor_scalar(
                                    scr[:], stag[sl][:], mBhn[:, mi:mi + 1], None, ALU.is_lt))
                                op("V", lambda e, sl=sl, mi=mi: e.tensor_scalar(
                                    stag[sl][:], stag[sl][:], mBh[:, mi:mi + 1], None, ALU.is_gt))
                                op("V", lambda e, sl=sl, tsb=tsb, half=half: e.tensor_tensor(
                                    tsb[:, half * 2048:(half + 1) * 2048], stag[sl][:],
                                    scr[:], ALU.subtract), tick="V")
                                stag_free[sl] = n["V"]
                            pool_tern[(mi, ob)] = n["V"]
                            wait("G", "V", pool_tern[(mi, ob)])
                            dma("G", "dG", tdram[ob], tsb[:])
                            dg_store[(mi, ob)] = n["dG"]
                        else:
                            if ob == 1 and mi == 0:
                                wait("S", "P", qrow_cons)
                            if ob >= 2:
                                wait("S", "P", gu_slabdone[(mi, ob - 2)])
                            dma("S", dsem, tsb[:], tdram[ob])
                            slab_ld[(mi, ob)] = n[dsem]

                emit_slab(0)
                for ob in range(nob):
                    if ob + 1 < nob:
                        emit_slab(ob + 1)
                    for ocl in range(4 if gu_mode >= 2 else 0):
                        oc = ob * 4 + ocl
                        for th in range(2):
                            hs = (oc * 2 + th) % 2
                            banks = [4 * hs + i for i in range(4)]
                            if call == 0:
                                wait("T", "V", pool_tern[(0, ob)])
                                wait("T", "V", pool_tern[(1, ob)])
                            else:
                                wait("T", "dSl0", slab_ld[(0, ob)])
                                wait("T", "dSl1", slab_ld[(1, ob)])
                            if gu_mode == 7:
                                wait("T", "V")
                            else:
                                wait("T", "V", max(evacV[hs],
                                                   *[evac_cnt[b] for b in banks]))
                            for mi in range(2):
                                tsb = tern_sbs[mi][ob % 2]
                                b0, b1 = banks[2 * mi], banks[2 * mi + 1]
                                for K in range(ND):
                                    lhs = tsb[:, K * 512 + ocl * 128: K * 512 + (ocl + 1) * 128]
                                    r0 = xqT[:, K * TPC + th * 1024: K * TPC + th * 1024 + 512]
                                    r1 = xqT[:, K * TPC + th * 1024 + 512: K * TPC + (th + 1) * 1024]
                                    op("T", lambda e, lhs=lhs, r0=r0, b0=b0, K=K: e.matmul(
                                        pb[b0][:], lhs, r0, start=(K == 0), stop=(K == ND - 1)))
                                    last = (mi == 1 and K == ND - 1)
                                    op("T", lambda e, lhs=lhs, r1=r1, b1=b1, K=K: e.matmul(
                                        pb[b1][:], lhs, r1, start=(K == 0), stop=(K == ND - 1)),
                                       tick="P" if last else None)
                            mmd = n["P"]
                            if ocl == 3 and th == 1:
                                gu_slabdone[(0, ob)] = mmd
                                gu_slabdone[(1, ob)] = mmd
                            if gu_mode < 3:
                                evacV[hs] = 0
                                continue
                            if gu_mode in (8, 9):
                                wait("V", "P", mmd)
                                for i2 in range(2):
                                    if gu_mode == 8:
                                        op("V", lambda e, b=banks[i2], i2=i2: e.tensor_copy(
                                            g2b[:, i2 * 512:(i2 + 1) * 512], pb[b][:]))
                                        op("V", lambda e, b=banks[2 + i2], i2=i2: e.tensor_copy(
                                            u2b[:, i2 * 512:(i2 + 1) * 512], pb[b][:]), tick="V")
                                    else:
                                        c0 = th * 1024 + i2 * 512
                                        op("V", lambda e, b=banks[i2], i2=i2, c0=c0: e.tensor_tensor(
                                            x_sb[:, i2 * 512:(i2 + 1) * 512], pb[b][:],
                                            sxB[:, c0:c0 + 512], ALU.mult))
                                        op("V", lambda e, b=banks[2 + i2], i2=i2, c0=c0: e.tensor_tensor(
                                            x_sb[:, 1024 + i2 * 512: 1024 + (i2 + 1) * 512], pb[b][:],
                                            sxB[:, c0:c0 + 512], ALU.mult), tick="V")
                                evacV[hs] = n["V"]
                                for b in banks:
                                    evac_cnt[b] = n["V"]
                                continue
                            wait("V", "P", mmd)
                            wait("V", "A", habs_done[hs])
                            for i2 in range(2):
                                c0 = th * 1024 + i2 * 512
                                op("V", lambda e, b=banks[i2], i2=i2, c0=c0: e.tensor_tensor(
                                    g2b[:, i2 * 512:(i2 + 1) * 512], pb[b][:],
                                    sxB[:, c0:c0 + 512], ALU.mult))
                            for i2 in range(2):
                                c0 = th * 1024 + i2 * 512
                                op("V", lambda e, b=banks[2 + i2], i2=i2, c0=c0: e.tensor_tensor(
                                    u2b[:, i2 * 512:(i2 + 1) * 512], pb[b][:],
                                    sxB[:, c0:c0 + 512], ALU.mult), tick="V")
                            evacV[hs] = n["V"]
                            for b in banks:
                                evac_cnt[b] = n["V"]
                            if gu_mode < 4 or gu_mode == 7:
                                continue
                            wait("A", "V", evacV[hs])
                            wait("A", "G", last_hmul_G)
                            op("A", lambda e: e.activation(out=sgb[:], in_=g2b[:], func=AF.Silu,
                                                           bias=0.0, scale=mB[:, 0:1]), tick="A")
                            if gu_mode < 5:
                                continue
                            wait("G", "A")
                            hsem = "dGh0" if hs == 0 else "dGh1"
                            wait("G", hsem, hstore[hs])
                            wait("G", "A", habs_done[hs])
                            hb = hbb[hs]
                            op("G", lambda e, hb=hb: e.tensor_tensor(hb[:], sgb[:], u2b[:], ALU.mult), tick="G")
                            last_hmul_G = n["G"]
                            dma("G", hsem, h_dram[oc, :, th * 1024:(th + 1) * 1024], hb[:])
                            hstore[hs] = n[hsem]
                            if gu_mode < 6:
                                continue
                            wait("A", "G")
                            op("A", lambda e, hb=hb: e.activation(out=habs[:], in_=hb[:],
                                                                  func=AF.Abs, bias=0.0, scale=1.0), tick="A")
                            habs_done[hs] = n["A"]
                            wait("V", "A", habs_done[hs])
                            op("V", lambda e, th=th: e.tensor_tensor(
                                mxt[:, th * 1024:(th + 1) * 1024],
                                mxt[:, th * 1024:(th + 1) * 1024], habs[:], ALU.max), tick="V")

            def phase_hscale():
                for tt in range(NT):
                    tb = 6 + (tt % 2)
                    wait("T", "V", evac_cnt[tb])
                    op("T", lambda e, tt=tt, tb=tb: e.transpose(
                        pb[tb][:, 0:128], mxt[:, tt * 128:(tt + 1) * 128], idf[:]), tick="P")
                    wait("V", "P")
                    op("V", lambda e, tt=tt, tb=tb: e.tensor_reduce(
                        st["maxh"][:, tt:tt + 1], pb[tb][:, 0:128], AX.X, ALU.max), tick="V",
                       drain=(tt == NT - 1))
                    evac_cnt[tb] = n["V"]
                op("V", lambda e: e.tensor_scalar(st["t16a"][:], st["maxh"][:], mB[:, 1:2],
                                                  EPS, ALU.mult, ALU.max), drain=True)
                op("V", lambda e: e.reciprocal(st["t16b"][:], st["t16a"][:]), drain=True)
                op("V", lambda e: e.tensor_scalar(st["sqh"][:], st["t16b"][:], mB[:, 1:2],
                                                  127.0, ALU.mult, ALU.mult), drain=True)
                op("V", lambda e: e.tensor_scalar(st["cdown"][:], st["t16a"][:], mB[:, 2:3],
                                                  -0.5 / 127.0, ALU.mult, ALU.mult), tick="V", drain=True)

            def phase_down(call):
                if call == 0:
                    wd_pool = {0: 0, 1: 0}
                    for c in range(NHC):
                        sl = c % 2
                        dsem = "dSl0" if sl == 0 else "dSl1"
                        wait("S", "V", wd_pool[sl])
                        dma("S", dsem, wdstag[sl][:], wdT3[c])
                        wait("V", dsem)
                        op("V", lambda e, sl=sl: e.tensor_scalar(
                            wdscr[:], wdstag[sl][:], mBhn[:, 2:3], None, ALU.is_lt))
                        op("V", lambda e, sl=sl: e.tensor_scalar(
                            wdstag[sl][:], wdstag[sl][:], mBh[:, 2:3], None, ALU.is_gt))
                        op("V", lambda e, sl=sl, c=c: e.tensor_tensor(
                            terndn[:, c * 1024:(c + 1) * 1024], wdstag[sl][:],
                            wdscr[:], ALU.subtract), tick="V")
                        wd_pool[sl] = n["V"]
                        wait("G", "V", wd_pool[sl])
                        dma("G", "dG", tern_d[c], terndn[:, c * 1024:(c + 1) * 1024])
                    wd_all_V = n["V"]
                else:
                    terndn3 = terndn[:, :].rearrange("p (c o) -> p c o", o=1024)
                    for g8 in range(4):
                        dma("S", "dS", terndn3[:, g8 * 8:(g8 + 1) * 8, :],
                            tern_d3[:, g8 * 8:(g8 + 1) * 8, :])
                    wd_all_V = n["V"]
                terndn_dS = n["dS"]
                pool_u = {0: 0, 1: 0}
                vch_done = {0: 0, 1: 0}
                evac_all = 0
                prev_down_p = 0
                for tq in range(4):
                    for i in range(4):
                        tt = tq * 4 + i
                        tb = 6 + (i % 2)
                        wait("T", "V", evac_cnt[tb])
                        op("T", lambda e, tt=tt, tb=tb: e.transpose(
                            pb[tb][0:1, 0:128], st["sqh"][:, tt:tt + 1], idf[:]), tick="P")
                        wait("V", "P")
                        op("V", lambda e, i=i, tb=tb: e.tensor_copy(
                            srow[0:1, i * 128:(i + 1) * 128], pb[tb][0:1, 0:128]), tick="V")
                        evac_cnt[tb] = n["V"]
                    wait("T", "V", evac_cnt[6])
                    op("T", lambda e: e.matmul(pb[6][:, 0:512], onesf[0:1, :], srow[0:1, 0:512],
                                               start=True, stop=True), tick="P")
                    wait("V", "P")
                    op("V", lambda e: e.tensor_copy(sBq[:], pb[6][:, 0:512]), tick="V")
                    evac_cnt[6] = n["V"]
                    sbq_ready = n["V"]
                    vch_tick = {}
                    for hc in range(NHC):
                        sl = hc % 2
                        hsem = "dH0" if sl == 0 else "dH1"
                        heng = "S" if sl == 0 else "A"
                        wait(heng, "G", pool_u[sl])
                        dma(heng, hsem, hrb[sl][:], h_dram[hc, :, tq * 512:(tq + 1) * 512])
                        wait("G", hsem)
                        wait("G", "V", sbq_ready)
                        wait("G", "V", vch_done[sl])
                        op("G", lambda e, sl=sl: e.tensor_tensor(uq5[sl][:], hrb[sl][:], sBq[:], ALU.mult), tick="G")
                        pool_u[sl] = n["G"]
                        wait("V", "G", pool_u[sl])
                        wait("V", "P", prev_down_p)
                        op("V", lambda e, sl=sl: e.tensor_copy(k5i[:], uq5[sl][:]))
                        op("V", lambda e, sl=sl: e.tensor_copy(kf5[sl][:], k5i[:]))
                        op("V", lambda e, sl=sl: e.tensor_tensor(v5[sl][:], uq5[sl][:], kf5[sl][:], ALU.subtract))
                        op("V", lambda e, sl=sl: e.scalar_tensor_tensor(
                            t15[sl][:], v5[sl][:], 0.5, kf5[sl][:], ALU.is_gt, ALU.add))
                        op("V", lambda e, sl=sl, hc=hc: e.scalar_tensor_tensor(
                            hq[:, hc * 512:(hc + 1) * 512], v5[sl][:], -0.5, t15[sl][:],
                            ALU.is_lt, ALU.subtract), tick="V")
                        vch_done[sl] = n["V"]
                        vch_tick[hc] = n["V"]
                    if call == 1:
                        wait("T", "dS", terndn_dS)
                    else:
                        wait("T", "V", wd_all_V)
                    wait("T", "V", evac_all)
                    for K in range(NHC):
                        wait("T", "V", vch_tick[K])
                        for i in range(4):
                            lhs = hq[:, K * 512 + i * 128: K * 512 + (i + 1) * 128]
                            op("T", lambda e, lhs=lhs, K=K, i=i: e.matmul(
                                pb[2 * i][:], lhs, terndn[:, K * 1024: K * 1024 + 512],
                                start=(K == 0), stop=(K == NHC - 1)))
                            op("T", lambda e, lhs=lhs, K=K, i=i: e.matmul(
                                pb[2 * i + 1][:], lhs, terndn[:, K * 1024 + 512: (K + 1) * 1024],
                                start=(K == 0), stop=(K == NHC - 1)),
                               tick="P" if (K == NHC - 1 and i == 3) else None)
                    prev_down_p = n["P"]
                    wait("V", "P", prev_down_p)
                    for i in range(4):
                        tt = tq * 4 + i
                        for j in range(2):
                            b = 2 * i + j
                            xsl = x_sb[:, tt * 1024 + j * 512: tt * 1024 + (j + 1) * 512]
                            op("V", lambda e, b=b, xsl=xsl, tt=tt: e.scalar_tensor_tensor(
                                xsl, pb[b][:], st["cdown"][:, tt:tt + 1], xsl, ALU.mult, ALU.add),
                               tick="V")
                    evac_all = n["V"]
                    for b in range(8):
                        evac_cnt[b] = n["V"]

            def ln_stats(dest):
                for tt in range(NT):
                    xt = x_sb[:, tt * 1024:(tt + 1) * 1024]
                    op("A", lambda e, xt=xt, tt=tt: e.activation(
                        out=dest[:], in_=xt, func=AF.Identity, bias=0.0,
                        scale=1.0, accum_out=st["macc2"][:, tt:tt + 1]), drain=True)
                tick("A", "A")
                wait("V", "A")
                op("V", lambda e: e.tensor_single_scalar(st["mu_neg"][:], st["macc2"][:], -1.0 / D, ALU.mult), tick="V", drain=True)
                wait("A", "V")
                for tt in range(NT):
                    xt = x_sb[:, tt * 1024:(tt + 1) * 1024]
                    op("A", lambda e, xt=xt, tt=tt: e.activation(
                        out=dest[:], in_=xt, func=AF.Square,
                        bias=st["mu_neg"][:, tt:tt + 1], scale=1.0,
                        accum_out=st["vacc"][:, tt:tt + 1]), drain=True)
                tick("A", "A")
                wait("V", "A")
                op("V", lambda e: e.tensor_single_scalar(st["t16a"][:], st["vacc"][:], 1.0 / D, ALU.mult), tick="V", drain=True)
                wait("A", "V")
                op("A", lambda e: e.activation(out=st["t16b"][:], in_=st["t16a"][:],
                                               func=AF.Sqrt, bias=epsc[:], scale=1.0), tick="A", drain=True)
                wait("V", "A")
                op("V", lambda e: e.reciprocal(st["inv"][:], st["t16b"][:]), drain=True)
                op("V", lambda e: e.tensor_tensor(st["cB"][:], st["mu_neg"][:], st["inv"][:], ALU.mult), tick="V", drain=True)

            def phase_kv():
                ln_stats(xlb)
                stats_v = n["V"]
                kvsl3 = kvslab[:, :].rearrange("p (c o) -> p c o", o=2048)
                for h2 in range(2):
                    dma("S", "dS", kvsl3[:, h2 * 4:(h2 + 1) * 4, :],
                        wkvT4[:, h2 * 4:(h2 + 1) * 4, :])
                dma("S", "dS", bkv_sb[:], bkvB)
                slab_dS = n["dS"]
                kv_evac_last = 0
                kv_evac = {0: 0, 1: 0}
                cmm = {0: 0, 1: 0}
                xl_cons = 0
                xlT_ev = {0: 0, 1: 0}
                kvmm_p = {0: 0, 1: 0}
                for tt in range(NT):
                    sl = tt % 2
                    if tt == 0:
                        wait("A", "V", stats_v)
                    wait("A", "P", xl_cons)
                    op("A", lambda e, tt=tt: e.activation(
                        out=xlb[:], in_=x_sb[:, tt * 1024:(tt + 1) * 1024], func=AF.Identity,
                        bias=st["cB"][:, tt:tt + 1], scale=st["inv"][:, tt:tt + 1]), tick="A")
                    wait("T", "A")
                    for dc in range(ND):
                        tb = 4 + (dc % 2)
                        wait("T", "V", evac_cnt[tb])
                        op("T", lambda e, dc=dc, tb=tb: e.transpose(
                            pb[tb][:, 0:128], xlb[:, dc * 128:(dc + 1) * 128], idf[:]), tick="P")
                        if dc == ND - 1:
                            xl_cons = n["P"]
                        wait("V", "P")
                        wait("V", "P", kvmm_p[sl])
                        op("V", lambda e, dc=dc, sl=sl, tb=tb: e.tensor_copy(
                            xlT[sl][:, dc * 128:(dc + 1) * 128], pb[tb][:, 0:128]), tick="V")
                        evac_cnt[tb] = n["V"]
                    xlT_ev[sl] = n["V"]
                    if tt == 0:
                        wait("T", "dS", slab_dS)
                    wait("T", "V", xlT_ev[sl])
                    wait("T", "V", kv_evac_last)
                    for K in range(ND):
                        for b in range(4):
                            op("T", lambda e, K=K, b=b, sl=sl: e.matmul(
                                pb[b][:], xlT[sl][:, K * 128:(K + 1) * 128],
                                kvslab[:, K * 2048 + b * 512: K * 2048 + (b + 1) * 512],
                                start=(K == 0), stop=(K == ND - 1)),
                               tick="P" if (K == ND - 1 and b == 3) else None)
                    kvmm_p[sl] = n["P"]
                    wait("V", "P", kvmm_p[sl])
                    wait("V", "P", cmm[sl])
                    for b in range(4):
                        op("V", lambda e, b=b, sl=sl: e.tensor_tensor(
                            kvt[sl][:, b * 512:(b + 1) * 512], pb[b][:],
                            bkv_sb[:, b * 512:(b + 1) * 512], ALU.add), tick="V")
                    kv_evac[sl] = n["V"]
                    kv_evac_last = n["V"]
                    wait("T", "V", kv_evac_last)
                    for h in range(HEADS):
                        cb = 6 + h // 4
                        cs = (h % 4) * 128
                        op("T", lambda e, h=h, sl=sl, cb=cb, cs=cs, tt=tt: e.matmul(
                            pb[cb][:, cs:cs + 128],
                            kvt[sl][:, h * 128:(h + 1) * 128],
                            kvt[sl][:, 1024 + h * 128: 1024 + (h + 1) * 128],
                            start=(tt == 0 and h % 4 == 0),
                            stop=(tt == NT - 1 and h % 4 == 3)),
                           tick="P" if h == HEADS - 1 else None)
                    cmm[sl] = n["P"]
                wait("V", "P")
                op("V", lambda e: e.tensor_copy(cpart[:, 0:512], pb[6][:]))
                op("V", lambda e: e.tensor_copy(cpart[:, 512:1024], pb[7][:]), tick="V")
                evac_cnt[6] = n["V"]
                evac_cnt[7] = n["V"]
                wait("G", "V")
                dma("G", "dG", cc_in, cpart[:])
                wait("G", "dG")
                if role == "G":
                    bump("CC", 1, E.collective_compute(
                        "AllReduce", ALU.add,
                        replica_groups=[[0, 1, 2, 3], [4, 5, 6, 7]],
                        ins=[cc_in], outs=[cc_out]))
                else:
                    bump("CC", 1)
                wait("G", "CC")
                dma("G", "dG", csb[:], cc_out)

            def phase_q():
                wq3 = wq_sb[:, :].rearrange("p (c o) -> p c o", o=1024)
                wo3 = wo_sb[:, :].rearrange("p (c o) -> p c o", o=1024)
                dma("S", "dS", wq3, wqT4)
                dma("S", "dS", wo3, woT4)
                dma("S", "dS", bq_sb[:], bqB)
                dma("S", "dS", bo_sb[:], boB)
                wdma = n["dS"]
                wait("G", "dG")
                tick("G", "G")
                csb_g = n["G"]
                q_evac = {0: 0, 1: 0}
                q_evac_last = 0
                att_evac = 0
                out_evac = 0
                xl_cons = 0
                xlT_ev = {0: 0, 1: 0}
                qmm_p = {0: 0, 1: 0}
                qtrans_p = {0: 0, 1: 0}
                womm_p = {0: 0, 1: 0}
                qt_done = {0: 0, 1: 0}
                for tt in range(NT):
                    sl = tt % 2
                    wait("A", "P", xl_cons)
                    wait("A", "V", out_evac)
                    op("A", lambda e, tt=tt: e.activation(
                        out=xlb7[:], in_=x_sb[:, tt * 1024:(tt + 1) * 1024], func=AF.Identity,
                        bias=st["cB"][:, tt:tt + 1], scale=st["inv"][:, tt:tt + 1]), tick="A")
                    wait("T", "A")
                    for dc in range(ND):
                        tb = 2 + (dc % 2)
                        wait("T", "V", evac_cnt[tb])
                        op("T", lambda e, dc=dc, tb=tb: e.transpose(
                            pb[tb][:, 0:128], xlb7[:, dc * 128:(dc + 1) * 128], idf[:]), tick="P")
                        if dc == ND - 1:
                            xl_cons = n["P"]
                        wait("V", "P")
                        wait("V", "P", qmm_p[sl])
                        op("V", lambda e, dc=dc, sl=sl, tb=tb: e.tensor_copy(
                            xlT7[sl][:, dc * 128:(dc + 1) * 128], pb[tb][:, 0:128]), tick="V")
                        evac_cnt[tb] = n["V"]
                    xlT_ev[sl] = n["V"]
                    if tt == 0:
                        wait("T", "dS", wdma)
                    wait("T", "V", xlT_ev[sl])
                    wait("T", "V", q_evac_last)
                    for K in range(ND):
                        for b in range(2):
                            op("T", lambda e, K=K, b=b, sl=sl: e.matmul(
                                pb[b][:], xlT7[sl][:, K * 128:(K + 1) * 128],
                                wq_sb[:, K * 1024 + b * 512: K * 1024 + (b + 1) * 512],
                                start=(K == 0), stop=(K == ND - 1)),
                               tick="P" if (K == ND - 1 and b == 1) else None)
                    qmm_p[sl] = n["P"]
                    wait("V", "P", qmm_p[sl])
                    wait("V", "P", qtrans_p[sl])
                    for b in range(2):
                        op("V", lambda e, b=b, sl=sl: e.tensor_tensor(
                            qtb[sl][:, b * 512:(b + 1) * 512], pb[b][:],
                            bq_sb[:, b * 512:(b + 1) * 512], ALU.add), tick="V")
                    q_evac[sl] = n["V"]
                    q_evac_last = n["V"]
                    wait("T", "V", q_evac_last)
                    qT = [qT_sb, qT_sb2][sl]
                    for h in range(HEADS):
                        tb = 2 + (h % 2)
                        wait("T", "V", evac_cnt[tb])
                        op("T", lambda e, h=h, sl=sl, tb=tb: e.transpose(
                            pb[tb][:, 0:128], qtb[sl][:, h * 128:(h + 1) * 128], idf[:]), tick="P")
                        wait("V", "P")
                        wait("V", "P", qt_done[sl])
                        op("V", lambda e, h=h, tb=tb, qT=qT: e.tensor_copy(
                            qT[:, h * 128:(h + 1) * 128], pb[tb][:, 0:128]), tick="V")
                        evac_cnt[tb] = n["V"]
                    qtrans_p[sl] = n["P"]
                    qtev = n["V"]
                    if tt == 0:
                        wait("T", "G", csb_g)
                    wait("T", "V", qtev)
                    wait("T", "V", att_evac)
                    for h in range(HEADS):
                        cb = 4 + h // 4
                        cs = (h % 4) * 128
                        op("T", lambda e, h=h, cb=cb, cs=cs, qT=qT: e.matmul(
                            pb[cb][:, cs:cs + 128], csb[:, h * 128:(h + 1) * 128],
                            qT[:, h * 128:(h + 1) * 128],
                            start=(h % 4 == 0), stop=(h % 4 == 3)),
                           tick="P" if h == HEADS - 1 else None)
                    qt_done[sl] = n["P"]
                    wait("V", "P", qt_done[sl])
                    wait("V", "P", womm_p[sl])
                    op("V", lambda e, sl=sl: e.tensor_copy(attT[sl][:, 0:512], pb[4][:]))
                    op("V", lambda e, sl=sl: e.tensor_copy(attT[sl][:, 512:1024], pb[5][:]), tick="V")
                    att_evac = n["V"]
                    wait("T", "V", att_evac)
                    wait("T", "V", out_evac)
                    for K in range(ND):
                        for b in range(2):
                            op("T", lambda e, K=K, b=b, sl=sl: e.matmul(
                                pb[6 + b][:], attT[sl][:, K * 128:(K + 1) * 128],
                                wo_sb[:, K * 1024 + b * 512: K * 1024 + (b + 1) * 512],
                                start=(K == 0), stop=(K == ND - 1)),
                               tick="P" if (K == ND - 1 and b == 1) else None)
                    womm = n["P"]
                    womm_p[sl] = womm
                    wait("V", "P", womm)
                    op("V", lambda e, tt=tt: e.tensor_tensor(
                        x_sb[:, tt * 1024:(tt + 1) * 1024],
                        x_sb[:, tt * 1024:(tt + 1) * 1024], bo_sb[:], ALU.add))
                    for b in range(2):
                        xsl = x_sb[:, tt * 1024 + b * 512: tt * 1024 + (b + 1) * 512]
                        op("V", lambda e, b=b, xsl=xsl: e.tensor_tensor(
                            xsl, xsl, pb[6 + b][:], ALU.add), tick="V")
                    out_evac = n["V"]

            steps = [phase_quant_x, lambda: phase_gateup(0), phase_hscale,
                     lambda: phase_down(0), phase_kv, phase_q,
                     phase_quant_x, lambda: phase_gateup(1), phase_hscale,
                     lambda: phase_down(1)]
            for si, step in enumerate(steps):
                if si < phases:
                    step()
                    barrier()
            for q4 in range(4):
                dma("S", "dS", y3[:, q4 * 4:(q4 + 1) * 4, :],
                    x_sb3[:, q4 * 4:(q4 + 1) * 4, :])
            if dbg:
                dma("S", "dS", o_tg, tern_g[0])
                dma("S", "dS", o_h, h_dram[0])
                dma("S", "dS", o_xq, xqT[:, 0:2048])
                dma("S", "dS", o_mb[:, 0:4], mB[:])
                dma("S", "dS", o_mb[:, 4:8], mBh[:])
                dma("S", "dS", o_mb[:, 8:12], mBhn[:])
                dma("S", "dS", o_st[:, 0:16], st["sqh"][:])
                dma("S", "dS", o_st[:, 16:32], st["maxh"][:])
                dma("S", "dS", o_st[:, 32:48], st["qv"][:])
                dma("S", "dS", o_st[:, 48:64], st["cdown"][:])
                dma("S", "dS", o_sx, sxB[:])
                dma("S", "dS", o_qr, qrow[0:1, :])
            wait("S", "dS")
            barrier()

        @block.sync
        def _(e):
            script("S", e)

        @block.gpsimd
        def _(e):
            script("G", e)

        @block.tensor
        def _(e):
            script("T", e)

        @block.vector
        def _(e):
            script("V", e)

        @block.scalar
        def _(e):
            script("A", e)

    return nc


def _host_inputs(L0_attn_x, L0_attn_w_qkv, L0_attn_b_qkv, L0_attn_w_o, L0_attn_b_o,
                 L0_ffn_w_gate, L0_ffn_w_up, L0_ffn_w_down):
    x = np.asarray(L0_attn_x, dtype=np.float32)
    wqkv = np.asarray(L0_attn_w_qkv, dtype=np.float32)
    bqkv = np.asarray(L0_attn_b_qkv, dtype=np.float32)
    wo = np.asarray(L0_attn_w_o, dtype=np.float32)
    bo = np.asarray(L0_attn_b_o, dtype=np.float32)
    wg = np.asarray(L0_ffn_w_gate, dtype=np.float32)
    wu = np.asarray(L0_ffn_w_up, dtype=np.float32)
    wd = np.asarray(L0_ffn_w_down, dtype=np.float32)

    c_, h_, dd_ = np.meshgrid(np.arange(3), np.arange(8), np.arange(128), indexing="ij")
    perm = (c_ * 1024 + dd_ * 8 + h_).reshape(-1)
    wqkvT = np.ascontiguousarray(wqkv.T[:, perm])
    bperm = bqkv[perm]

    common = {
        "wkvT": np.ascontiguousarray(wqkvT[:, 1024:3072]),
        "wqT": np.ascontiguousarray(wqkvT[:, 0:1024]),
        "woT": np.ascontiguousarray(wo.T),
        "wgT": np.ascontiguousarray(
            wg.T.reshape(8, 128, 8, 512).transpose(2, 1, 0, 3).reshape(8, 128, 4096)),
        "wuT": np.ascontiguousarray(
            wu.T.reshape(8, 128, 8, 512).transpose(2, 1, 0, 3).reshape(8, 128, 4096)),
        "wdT": np.ascontiguousarray(wd.T),
        "bkvB": np.ascontiguousarray(np.broadcast_to(bperm[1024:3072], (128, 2048))),
        "bqB": np.ascontiguousarray(np.broadcast_to(bperm[0:1024], (128, 1024))),
        "boB": np.ascontiguousarray(np.broadcast_to(bo, (128, 1024))),
        "idf_in": np.eye(128, dtype=np.float32),
        "ones_in": np.ones((128, 128), dtype=np.float32),
        "eps_in": np.full((128, 1), 1e-5, dtype=np.float32),
    }
    in_maps = []
    for core in range(8):
        b, tc = core // 4, core % 4
        m = dict(common)
        m["x_in"] = np.ascontiguousarray(x[b, tc * TPC:(tc + 1) * TPC, :])
        in_maps.append(m)
    return in_maps


def kernel(**inputs):
    in_maps = _host_inputs(**inputs)
    nc = build_nc()
    res = run_bass_kernel_spmd(nc, in_maps, core_ids=list(range(8)))
    out = np.empty((2, 8192, 1024), dtype=np.float32)
    for core in range(8):
        b, tc = core // 4, core % 4
        out[b, tc * TPC:(tc + 1) * TPC, :] = res.results[core]["y_out"]
    return out


# version 13
# speedup vs baseline: 1.0304x; 1.0304x over previous
import contextlib
import numpy as np
import concourse.bass as bass
import concourse.mybir as mybir
from concourse.bass_utils import run_bass_kernel_spmd

F32 = mybir.dt.float32
BF16 = mybir.dt.bfloat16
I8 = mybir.dt.int8
I32 = mybir.dt.int32
AF = mybir.ActivationFunctionType
ALU = mybir.AluOpType
AX = mybir.AxisListType

TPC = 2048
NT = 16
D = 1024
ND = 8
NI = 4096
NHC = 32
NOB = 8
HEADS = 8
EPS = 1e-5
NW = float(NI * D)

ENGINES = ("S", "G", "T", "V", "A")
SEMS = ("dS", "dSl0", "dSl1", "dH0", "dH1", "dG", "dC0", "dC1", "dGh0", "dGh1",
        "P", "V", "A", "G", "CC", "bS", "bG", "bT", "bV", "bA")

OVB = 102400


def build_nc(phases=12, gu_limit=99, gu_mode=6, dbg=False, q_serial=False):
    nc = bass.Bass("TRN2", target_bir_lowering=False, debug=False,
                   detect_race_conditions=False)

    x_in = nc.dram_tensor("x_in", [TPC, D], F32, kind="ExternalInput").ap()
    wkvT = nc.dram_tensor("wkvT", [D, 2048], F32, kind="ExternalInput").ap()
    wqT_d = nc.dram_tensor("wqT", [D, 1024], F32, kind="ExternalInput").ap()
    woT_d = nc.dram_tensor("woT", [D, 1024], F32, kind="ExternalInput").ap()
    wgT = nc.dram_tensor("wgT", [NOB, 128, 4096], F32, kind="ExternalInput").ap()
    wuT = nc.dram_tensor("wuT", [NOB, 128, 4096], F32, kind="ExternalInput").ap()
    wdT = nc.dram_tensor("wdT", [NI, D], F32, kind="ExternalInput").ap()
    bkvB = nc.dram_tensor("bkvB", [128, 2048], F32, kind="ExternalInput").ap()
    bqB = nc.dram_tensor("bqB", [128, 1024], F32, kind="ExternalInput").ap()
    boB = nc.dram_tensor("boB", [128, 1024], F32, kind="ExternalInput").ap()
    idf_in = nc.dram_tensor("idf_in", [128, 128], F32, kind="ExternalInput").ap()
    ones_in = nc.dram_tensor("ones_in", [128, 128], F32, kind="ExternalInput").ap()
    eps_in = nc.dram_tensor("eps_in", [128, 1], F32, kind="ExternalInput").ap()
    y_out = nc.dram_tensor("y_out", [TPC, D], F32, kind="ExternalOutput").ap()
    if dbg:
        o_tg = nc.dram_tensor("o_tg", [128, 4096], BF16, kind="ExternalOutput").ap()
        o_h = nc.dram_tensor("o_h", [128, 2048], F32, kind="ExternalOutput").ap()
        o_xq = nc.dram_tensor("o_xq", [128, 2048], BF16, kind="ExternalOutput").ap()
        o_mb = nc.dram_tensor("o_mb", [128, 12], F32, kind="ExternalOutput").ap()
        o_st = nc.dram_tensor("o_st", [128, 64], F32, kind="ExternalOutput").ap()
        o_sx = nc.dram_tensor("o_sx", [128, 2048], F32, kind="ExternalOutput").ap()
        o_qr = nc.dram_tensor("o_qr", [1, 2048], F32, kind="ExternalOutput").ap()

    h_dram = nc.dram_tensor("h_dram", [NHC, 128, TPC], F32).ap()
    tern_g = nc.dram_tensor("tern_g", [NOB, 128, 4096], BF16).ap()
    tern_u = nc.dram_tensor("tern_u", [NOB, 128, 4096], BF16).ap()
    tern_d = nc.dram_tensor("tern_d", [NHC, 128, 1024], BF16).ap()
    cc_in = nc.dram_tensor("cc_in", [128, 1024], F32).ap()
    cc_out = nc.dram_tensor("cc_out", [128, 1024], F32).ap()

    x_sb = nc.alloc_sbuf_tensor("x_sb", [128, NT * 1024], F32)
    idf = nc.alloc_sbuf_tensor("idf", [128, 128], F32)
    onesf = nc.alloc_sbuf_tensor("onesf", [128, 128], F32)
    epsc = nc.alloc_sbuf_tensor("epsc", [128, 1], F32)
    csb = nc.alloc_sbuf_tensor("csb", [128, 1024], F32)
    cpart = nc.alloc_sbuf_tensor("cpart", [128, 1024], F32)
    st_names = ["macc2", "vacc", "mu_neg", "inv", "rmx", "rmn", "c1n", "cB",
                "qv", "maxh", "cdown", "sqh", "t16a", "t16b"]
    st = {nm: nc.alloc_sbuf_tensor("st_" + nm, [128, 16], F32) for nm in st_names}
    macc = nc.alloc_sbuf_tensor("macc", [128, 64], F32)
    mrow = nc.alloc_sbuf_tensor("mrow", [128, 4], F32)
    msc2 = nc.alloc_sbuf_tensor("msc2", [1, 4], F32)
    mB = nc.alloc_sbuf_tensor("mB", [128, 4], F32)
    mBh = nc.alloc_sbuf_tensor("mBh", [128, 4], F32)
    mBhn = nc.alloc_sbuf_tensor("mBhn", [128, 4], F32)

    def ov(name, shape, dtype, off):
        assert off >= 0
        return nc.alloc_sbuf_tensor_at(name, shape, dtype, offset=OVB + off)

    xqT = ov("xqT", [128, ND * TPC], BF16, 0)
    uq = [ov(f"uq{i}", [128, 1024], F32, 32768 + i * 4096) for i in range(2)]
    kfb = [ov(f"kfb{i}", [128, 1024], F32, 40960 + i * 4096) for i in range(2)]
    kqf = [ov(f"kqf{i}", [128, 1024], F32, 49152 + i * 4096) for i in range(2)]
    vres = [ov(f"vres{i}", [128, 1024], F32, 57344 + i * 4096) for i in range(2)]
    k8b = [ov(f"k8b{i}", [128, 1024], I8, 65536 + i * 1024) for i in range(2)]
    scr = ov("scr", [128, 2048], F32, 67584)
    xqi = ov("xqi", [128, 1024], I32, 67584)
    stag = [ov(f"stag{i}", [128, 2048], F32, 75776 + i * 8192) for i in range(2)]
    terng_sb = ov("terng_sb", [128, 4096], BF16, 92160)
    ternu_sb = ov("ternu_sb", [128, 4096], BF16, 100352)
    terng_sb2 = ov("terng_sb2", [128, 4096], BF16, 53248)
    ternu_sb2 = nc.alloc_sbuf_tensor_at("ternu_sb2", [128, 4096], BF16, offset=94208)
    sgb = ov("sgb", [128, 1024], F32, 61440)
    g2b = ov("g2b", [128, 1024], F32, 108544)
    u2b = ov("u2b", [128, 1024], F32, 112640)
    hbb = [ov(f"hbb{i}", [128, 1024], F32, 116736 + i * 4096) for i in range(2)]
    habs = ov("habs", [128, 1024], F32, 32768)
    sxB = ov("sxB", [128, 2048], F32, 36864)
    mxt = ov("mxt", [128, 2048], F32, 45056)
    qrow = ov("qrow", [1, 2048], F32, 53248)
    terndn = ov("terndn", [128, NHC * 1024], BF16, 0)
    wdstag = [ov(f"wdstag{i}", [128, 1024], F32, 65536 + i * 4096) for i in range(2)]
    wdscr = ov("wdscr", [128, 1024], F32, 73728)
    hq = ov("hq", [128, NHC * 512], BF16, 65536)
    hrb = [ov(f"hrb{i}", [128, 512], F32, 98304 + i * 2048) for i in range(2)]
    uq5 = [ov(f"uq5{i}", [128, 512], F32, 102400 + i * 2048) for i in range(2)]
    kf5 = [ov(f"kf5{i}", [128, 512], F32, 106496 + i * 2048) for i in range(2)]
    k85 = [ov(f"k85{i}", [128, 512], I8, 110592 + i * 512) for i in range(2)]
    v5 = [ov(f"v5{i}", [128, 512], F32, 111616 + i * 2048) for i in range(2)]
    t15 = [ov(f"t15{i}", [128, 512], F32, 115712 + i * 2048) for i in range(2)]
    sBq = ov("sBq", [128, 512], F32, 119808)
    srow = ov("srow", [1, 512], F32, 121856)
    k5i = ov("k5i", [128, 512], I32, 123904)
    kvslab = ov("kvslab", [128, ND * 2048], F32, 0)
    kvt = [ov(f"kvt{i}", [128, 2048], F32, 65536 + i * 8192) for i in range(2)]
    xlb = ov("xlb", [128, 1024], F32, 81920)
    xlT = [ov(f"xlT{i}", [128, 1024], F32, 86016 + i * 4096) for i in range(2)]
    bkv_sb = ov("bkv_sb", [128, 2048], F32, 94208)
    wq_sb = ov("wq_sb", [128, ND * 1024], F32, 0)
    wo_sb = ov("wo_sb", [128, ND * 1024], F32, 32768)
    qtb = [ov(f"qtb{i}", [128, 1024], F32, 65536 + i * 4096) for i in range(2)]
    qT_sb = ov("qT_sb", [128, 1024], F32, 73728)
    qT_sb2 = ov("qT_sb2", [128, 1024], F32, 106496)
    attT = [ov(f"attT{i}", [128, 1024], F32, 77824 + i * 4096) for i in range(2)]
    xlb7 = ov("xlb7", [128, 1024], F32, 86016)
    xlT7 = [ov(f"xlT7{i}", [128, 1024], F32, 90112 + i * 4096) for i in range(2)]
    bq_sb = ov("bq_sb", [128, 1024], F32, 98304)
    bo_sb = ov("bo_sb", [128, 1024], F32, 102400)

    pb = [nc.alloc_psum_tensor(f"pb{i}", [128, 512], F32) for i in range(8)]

    x_in3 = x_in.rearrange("(t p) d -> p t d", p=128)
    y3 = y_out.rearrange("(t p) d -> p t d", p=128)
    wdT3 = wdT.rearrange("(c p) o -> c p o", p=128)
    wkvT4 = wkvT.rearrange("(c p) o -> p c o", p=128)
    wqT4 = wqT_d.rearrange("(c p) o -> p c o", p=128)
    woT4 = woT_d.rearrange("(c p) o -> p c o", p=128)
    tern_d3 = tern_d.rearrange("c p o -> p c o")

    with contextlib.ExitStack() as stack:
        SEMH = {s: stack.enter_context(nc.semaphore(s)) for s in SEMS}
        block = stack.enter_context(nc.Block())

        def script(role, E):
            n = {s: 0 for s in SEMS}
            evac_cnt = {i: 0 for i in range(8)}

            def bump(sem, k, inst=None):
                n[sem] += k
                if inst is not None:
                    inst.then_inc(SEMH[sem], k)

            def wait(eng, sem, val=None):
                if role == eng:
                    E.wait_ge(SEMH[sem], n[sem] if val is None else val)

            def dma(eng, sem, out, in_):
                if role == eng:
                    bump(sem, 16, E.dma_start(out=out, in_=in_))
                else:
                    bump(sem, 16)

            def op(eng, fn, tick=None, drain=False):
                if role == eng:
                    inst = fn(E)
                    if tick is not None:
                        bump(tick, 1, inst)
                    if drain:
                        E.drain()
                elif tick is not None:
                    bump(tick, 1)

            def tick(eng, sem):
                if role == eng:
                    bump(sem, 1, E.nop())
                else:
                    bump(sem, 1)

            def barrier():
                if role == "S":
                    for s in ("dS", "dSl0", "dSl1", "dH0", "dH1"):
                        E.wait_ge(SEMH[s], n[s])
                if role == "G":
                    for s in ("dG", "dC0", "dC1", "dGh0", "dGh1", "CC"):
                        E.wait_ge(SEMH[s], n[s])
                own = {"T": "P", "V": "V", "A": "A", "G": "G"}
                if role in own:
                    E.wait_ge(SEMH[own[role]], n[own[role]])
                bar = {"S": "bS", "G": "bG", "T": "bT", "V": "bV", "A": "bA"}
                for eng in ENGINES:
                    if role == eng:
                        bump(bar[eng], 1, E.nop())
                    else:
                        bump(bar[eng], 1)
                for eng in ENGINES:
                    if eng != role:
                        E.wait_ge(SEMH[bar[eng]], n[bar[eng]])

            x_sb3 = x_sb[:, :].rearrange("p (t d) -> p t d", d=1024)
            for q4 in range(4):
                dma("S", "dS", x_sb3[:, q4 * 4:(q4 + 1) * 4, :],
                    x_in3[:, q4 * 4:(q4 + 1) * 4, :])
            dma("S", "dS", idf[:], idf_in)
            dma("S", "dS", onesf[:], ones_in)
            dma("S", "dS", epsc[:], eps_in)

            scan_a = {0: 0, 1: 0}
            pieces = []
            for c in range(NOB):
                pieces.append((wgT, c, 0, 2048, 0 + 2 * c))
                pieces.append((wgT, c, 2048, 2048, 1 + 2 * c))
            for c in range(NOB):
                pieces.append((wuT, c, 0, 2048, 16 + 2 * c))
                pieces.append((wuT, c, 2048, 2048, 17 + 2 * c))
            for c in range(NHC):
                pieces.append((wdT3, c, 0, 1024, 32 + c))
            for ci, (src, c, o0, w, col) in enumerate(pieces):
                sl = ci % 2
                dsem = "dSl0" if sl == 0 else "dSl1"
                wait("S", "A", scan_a[sl])
                dma("S", dsem, stag[sl][:, 0:w], src[c][:, o0:o0 + w])
                wait("A", dsem)
                op("A", lambda e, sl=sl, w=w, col=col: e.activation(
                    out=scr[:, 0:w], in_=stag[sl][:, 0:w], func=AF.Abs, bias=0.0,
                    scale=1.0, accum_out=macc[:, col:col + 1]), tick="A", drain=True)
                scan_a[sl] = n["A"]
            wait("V", "A")
            op("V", lambda e: e.tensor_reduce(mrow[:, 0:1], macc[:, 0:16], AX.X, ALU.add))
            op("V", lambda e: e.tensor_reduce(mrow[:, 1:2], macc[:, 16:32], AX.X, ALU.add))
            op("V", lambda e: e.tensor_reduce(mrow[:, 2:3], macc[:, 32:64], AX.X, ALU.add), tick="V", drain=True)
            wait("T", "dS", 112)
            wait("T", "V")
            op("T", lambda e: e.matmul(pb[0][0:1, 0:3], onesf[:, 0:1], mrow[:, 0:3],
                                       start=True, stop=True), tick="P")
            wait("V", "P")
            op("V", lambda e: e.tensor_scalar(msc2[0:1, 0:3], pb[0][0:1, 0:3],
                                              1.0 / NW, EPS, ALU.mult, ALU.max), tick="V", drain=True)
            wait("T", "V")
            op("T", lambda e: e.matmul(pb[1][:, 0:3], onesf[0:1, :], msc2[0:1, 0:3],
                                       start=True, stop=True), tick="P")
            wait("V", "P")
            op("V", lambda e: e.tensor_copy(mB[:, 0:3], pb[1][:, 0:3]), drain=True)
            op("V", lambda e: e.tensor_single_scalar(mBhn[:, 0:3], mB[:, 0:3], -0.5, ALU.mult), drain=True)
            op("V", lambda e: e.tensor_single_scalar(mBh[:, 0:3], mB[:, 0:3], 0.5, ALU.mult), tick="V", drain=True)

            def phase_quant_x():
                wait("A", "dS", 112)
                for tt in range(NT):
                    xt = x_sb[:, tt * 1024:(tt + 1) * 1024]
                    op("A", lambda e, xt=xt, tt=tt: e.activation(
                        out=scr[:, 0:1024], in_=xt, func=AF.Identity, bias=0.0,
                        scale=1.0, accum_out=st["macc2"][:, tt:tt + 1]), drain=True)
                tick("A", "A")
                wait("V", "A")
                op("V", lambda e: e.tensor_single_scalar(st["mu_neg"][:], st["macc2"][:], -1.0 / D, ALU.mult), tick="V", drain=True)
                wait("A", "V")
                for tt in range(NT):
                    xt = x_sb[:, tt * 1024:(tt + 1) * 1024]
                    op("A", lambda e, xt=xt, tt=tt: e.activation(
                        out=scr[:, 0:1024], in_=xt, func=AF.Square,
                        bias=st["mu_neg"][:, tt:tt + 1], scale=1.0,
                        accum_out=st["vacc"][:, tt:tt + 1]), drain=True)
                tick("A", "A")
                wait("V", "A")
                op("V", lambda e: e.tensor_single_scalar(st["t16a"][:], st["vacc"][:], 1.0 / D, ALU.mult), tick="V", drain=True)
                wait("A", "V")
                op("A", lambda e: e.activation(out=st["t16b"][:], in_=st["t16a"][:],
                                               func=AF.Sqrt, bias=epsc[:], scale=1.0), tick="A", drain=True)
                wait("V", "A")
                op("V", lambda e: e.reciprocal(st["inv"][:], st["t16b"][:]), drain=True)
                for tt in range(NT):
                    xt = x_sb[:, tt * 1024:(tt + 1) * 1024]
                    op("V", lambda e, xt=xt, tt=tt: e.tensor_reduce(st["rmx"][:, tt:tt + 1], xt, AX.X, ALU.max))
                    op("V", lambda e, xt=xt, tt=tt: e.tensor_reduce(st["rmn"][:, tt:tt + 1], xt, AX.X, ALU.min),
                       drain=(tt == NT - 1))
                op("V", lambda e: e.tensor_tensor(st["t16a"][:], st["rmx"][:], st["mu_neg"][:], ALU.add), drain=True)
                op("V", lambda e: e.tensor_tensor(st["t16b"][:], st["rmn"][:], st["mu_neg"][:], ALU.add), drain=True)
                op("V", lambda e: e.scalar_tensor_tensor(st["t16a"][:], st["t16b"][:], -1.0,
                                                         st["t16a"][:], ALU.mult, ALU.max), drain=True)
                op("V", lambda e: e.tensor_tensor(st["t16a"][:], st["t16a"][:], st["inv"][:], ALU.mult), drain=True)
                op("V", lambda e: e.tensor_scalar_max(st["t16a"][:], st["t16a"][:], EPS), drain=True)
                op("V", lambda e: e.tensor_single_scalar(st["qv"][:], st["t16a"][:], 1.0 / 127.0, ALU.mult), drain=True)
                op("V", lambda e: e.reciprocal(st["t16b"][:], st["t16a"][:]), drain=True)
                op("V", lambda e: e.tensor_tensor(st["t16b"][:], st["t16b"][:], st["inv"][:], ALU.mult), drain=True)
                op("V", lambda e: e.tensor_single_scalar(st["c1n"][:], st["t16b"][:], -127.0, ALU.mult), drain=True)
                op("V", lambda e: e.tensor_tensor(st["cB"][:], st["mu_neg"][:], st["c1n"][:], ALU.mult), tick="V", drain=True)
                stats_v = n["V"]
                wait("A", "V", stats_v)
                vq_done = {0: 0, 1: 0}
                tq_done = {0: 0, 1: 0}
                for tt in range(NT):
                    sl = tt % 2
                    if tt >= 2:
                        wait("A", "V", vq_done[sl])
                    xt = x_sb[:, tt * 1024:(tt + 1) * 1024]
                    op("A", lambda e, xt=xt, tt=tt, sl=sl: e.activation(
                        out=uq[sl][:], in_=xt, func=AF.Identity,
                        bias=st["cB"][:, tt:tt + 1], scale=st["c1n"][:, tt:tt + 1]), tick="A")
                    wait("V", "A")
                    wait("V", "P", tq_done[sl])
                    op("V", lambda e, sl=sl: e.tensor_copy(xqi[:], uq[sl][:]))
                    op("V", lambda e, sl=sl: e.tensor_copy(kfb[sl][:], xqi[:]))
                    op("V", lambda e, sl=sl: e.tensor_tensor(vres[sl][:], uq[sl][:], kfb[sl][:], ALU.subtract))
                    op("V", lambda e, sl=sl: e.scalar_tensor_tensor(kqf[sl][:], vres[sl][:], 0.5,
                                                                    kfb[sl][:], ALU.is_gt, ALU.add))
                    op("V", lambda e, sl=sl: e.scalar_tensor_tensor(kqf[sl][:], vres[sl][:], -0.5,
                                                                    kqf[sl][:], ALU.is_lt, ALU.subtract), tick="V")
                    vq_done[sl] = n["V"]
                    wait("T", "V", vq_done[sl])
                    for dc in range(ND):
                        tb = 6 + (dc % 2)
                        wait("T", "V", evac_cnt[tb])
                        op("T", lambda e, dc=dc, sl=sl, tb=tb: e.transpose(
                            pb[tb][:, 0:128], kqf[sl][:, dc * 128:(dc + 1) * 128], idf[:]), tick="P")
                        wait("V", "P")
                        op("V", lambda e, dc=dc, tt=tt, tb=tb: e.tensor_copy(
                            xqT[:, dc * TPC + tt * 128: dc * TPC + (tt + 1) * 128],
                            pb[tb][:, 0:128]), tick="V")
                        evac_cnt[tb] = n["V"]
                    tq_done[sl] = n["P"]
                    if q_serial:
                        barrier()

            def phase_gateup(call, gu_limit=gu_limit, gu_mode=gu_mode):
                op("G", lambda e: e.memset(mxt[:], 0.0), tick="G")
                mxt_zeroed = n["G"]
                wait("V", "G", mxt_zeroed)
                for tt in range(NT):
                    tb = 6 + (tt % 2)
                    wait("T", "V", evac_cnt[tb])
                    op("T", lambda e, tt=tt, tb=tb: e.transpose(
                        pb[tb][0:1, 0:128], st["qv"][:, tt:tt + 1], idf[:]), tick="P")
                    wait("V", "P")
                    op("V", lambda e, tt=tt, tb=tb: e.tensor_copy(
                        qrow[0:1, tt * 128:(tt + 1) * 128], pb[tb][0:1, 0:128]), tick="V")
                    evac_cnt[tb] = n["V"]
                for b in range(4):
                    tb = 6 + (b % 2)
                    wait("T", "V", evac_cnt[tb])
                    op("T", lambda e, b=b, tb=tb: e.matmul(
                        pb[tb][:, 0:512], onesf[0:1, :], qrow[0:1, b * 512:(b + 1) * 512],
                        start=True, stop=True), tick="P")
                    wait("V", "P")
                    op("V", lambda e, b=b, tb=tb: e.tensor_copy(
                        sxB[:, b * 512:(b + 1) * 512], pb[tb][:, 0:512]), tick="V")
                    evac_cnt[tb] = n["V"]
                qrow_cons = n["P"]
                if gu_limit == 0:
                    return
                stag3 = [s[:, :].rearrange("p (c o) -> p c o", o=512) for s in stag]
                tern_sbs = [[terng_sb, terng_sb2], [ternu_sb, ternu_sb2]]
                gu_slabdone = {}
                pool_tern = {}
                slab_ld = {}
                stag_free = {0: 0, 1: 0}
                evacV = {0: 0, 1: 0}
                habs_done = {0: 0, 1: 0}
                hstore = {0: 0, 1: 0}
                last_hmul_G = 0
                dg_store = {}
                nob = min(NOB, gu_limit)

                def emit_slab(ob):
                    for mi in range(2):
                        wsrc = [wgT, wuT][mi]
                        tdram = [tern_g, tern_u][mi]
                        tsb = tern_sbs[mi][ob % 2]
                        dsem = "dSl0" if mi == 0 else "dSl1"
                        if call == 0:
                            for half in range(2):
                                sl = mi
                                wait("S", "V", stag_free[sl])
                                dma("S", dsem, stag[sl][:],
                                    wsrc[ob][:, half * 2048:(half + 1) * 2048])
                                wait("V", dsem)
                                if half == 0:
                                    if ob == 1 and mi == 0:
                                        wait("V", "P", qrow_cons)
                                    if ob >= 2:
                                        wait("V", "P", gu_slabdone[(mi, ob - 2)])
                                        wait("V", "dG", dg_store[(mi, ob - 2)])
                                op("V", lambda e, sl=sl, mi=mi: e.tensor_scalar(
                                    scr[:], stag[sl][:], mBhn[:, mi:mi + 1], None, ALU.is_lt))
                                op("V", lambda e, sl=sl, mi=mi: e.tensor_scalar(
                                    stag[sl][:], stag[sl][:], mBh[:, mi:mi + 1], None, ALU.is_gt))
                                op("V", lambda e, sl=sl, tsb=tsb, half=half: e.tensor_tensor(
                                    tsb[:, half * 2048:(half + 1) * 2048], stag[sl][:],
                                    scr[:], ALU.subtract), tick="V")
                                stag_free[sl] = n["V"]
                            pool_tern[(mi, ob)] = n["V"]
                            wait("G", "V", pool_tern[(mi, ob)])
                            dma("G", "dG", tdram[ob], tsb[:])
                            dg_store[(mi, ob)] = n["dG"]
                        else:
                            if ob == 1 and mi == 0:
                                wait("S", "P", qrow_cons)
                            if ob >= 2:
                                wait("S", "P", gu_slabdone[(mi, ob - 2)])
                            dma("S", dsem, tsb[:], tdram[ob])
                            slab_ld[(mi, ob)] = n[dsem]

                emit_slab(0)
                for ob in range(nob):
                    if ob + 1 < nob:
                        emit_slab(ob + 1)
                    for ocl in range(4 if gu_mode >= 2 else 0):
                        oc = ob * 4 + ocl
                        for th in range(2):
                            hs = (oc * 2 + th) % 2
                            banks = [4 * hs + i for i in range(4)]
                            if call == 0:
                                wait("T", "V", pool_tern[(0, ob)])
                                wait("T", "V", pool_tern[(1, ob)])
                            else:
                                wait("T", "dSl0", slab_ld[(0, ob)])
                                wait("T", "dSl1", slab_ld[(1, ob)])
                            if gu_mode == 7:
                                wait("T", "V")
                            else:
                                wait("T", "V", max(evacV[hs],
                                                   *[evac_cnt[b] for b in banks]))
                            for mi in range(2):
                                tsb = tern_sbs[mi][ob % 2]
                                b0, b1 = banks[2 * mi], banks[2 * mi + 1]
                                for K in range(ND):
                                    lhs = tsb[:, K * 512 + ocl * 128: K * 512 + (ocl + 1) * 128]
                                    r0 = xqT[:, K * TPC + th * 1024: K * TPC + th * 1024 + 512]
                                    r1 = xqT[:, K * TPC + th * 1024 + 512: K * TPC + (th + 1) * 1024]
                                    op("T", lambda e, lhs=lhs, r0=r0, b0=b0, K=K: e.matmul(
                                        pb[b0][:], lhs, r0, start=(K == 0), stop=(K == ND - 1)))
                                    last = (mi == 1 and K == ND - 1)
                                    op("T", lambda e, lhs=lhs, r1=r1, b1=b1, K=K: e.matmul(
                                        pb[b1][:], lhs, r1, start=(K == 0), stop=(K == ND - 1)),
                                       tick="P" if last else None)
                            mmd = n["P"]
                            if ocl == 3 and th == 1:
                                gu_slabdone[(0, ob)] = mmd
                                gu_slabdone[(1, ob)] = mmd
                            if gu_mode < 3:
                                evacV[hs] = 0
                                continue
                            if gu_mode in (8, 9):
                                wait("V", "P", mmd)
                                for i2 in range(2):
                                    if gu_mode == 8:
                                        op("V", lambda e, b=banks[i2], i2=i2: e.tensor_copy(
                                            g2b[:, i2 * 512:(i2 + 1) * 512], pb[b][:]))
                                        op("V", lambda e, b=banks[2 + i2], i2=i2: e.tensor_copy(
                                            u2b[:, i2 * 512:(i2 + 1) * 512], pb[b][:]), tick="V")
                                    else:
                                        c0 = th * 1024 + i2 * 512
                                        op("V", lambda e, b=banks[i2], i2=i2, c0=c0: e.tensor_tensor(
                                            x_sb[:, i2 * 512:(i2 + 1) * 512], pb[b][:],
                                            sxB[:, c0:c0 + 512], ALU.mult))
                                        op("V", lambda e, b=banks[2 + i2], i2=i2, c0=c0: e.tensor_tensor(
                                            x_sb[:, 1024 + i2 * 512: 1024 + (i2 + 1) * 512], pb[b][:],
                                            sxB[:, c0:c0 + 512], ALU.mult), tick="V")
                                evacV[hs] = n["V"]
                                for b in banks:
                                    evac_cnt[b] = n["V"]
                                continue
                            wait("V", "P", mmd)
                            wait("V", "A", habs_done[hs])
                            for i2 in range(2):
                                c0 = th * 1024 + i2 * 512
                                op("V", lambda e, b=banks[i2], i2=i2, c0=c0: e.tensor_tensor(
                                    g2b[:, i2 * 512:(i2 + 1) * 512], pb[b][:],
                                    sxB[:, c0:c0 + 512], ALU.mult))
                            for i2 in range(2):
                                c0 = th * 1024 + i2 * 512
                                op("V", lambda e, b=banks[2 + i2], i2=i2, c0=c0: e.tensor_tensor(
                                    u2b[:, i2 * 512:(i2 + 1) * 512], pb[b][:],
                                    sxB[:, c0:c0 + 512], ALU.mult), tick="V")
                            evacV[hs] = n["V"]
                            for b in banks:
                                evac_cnt[b] = n["V"]
                            if gu_mode < 4 or gu_mode == 7:
                                continue
                            wait("A", "V", evacV[hs])
                            wait("A", "G", last_hmul_G)
                            op("A", lambda e: e.activation(out=sgb[:], in_=g2b[:], func=AF.Silu,
                                                           bias=0.0, scale=mB[:, 0:1]), tick="A")
                            if gu_mode < 5:
                                continue
                            wait("G", "A")
                            hsem = "dGh0" if hs == 0 else "dGh1"
                            wait("G", hsem, hstore[hs])
                            wait("G", "A", habs_done[hs])
                            hb = hbb[hs]
                            op("G", lambda e, hb=hb: e.tensor_tensor(hb[:], sgb[:], u2b[:], ALU.mult), tick="G")
                            last_hmul_G = n["G"]
                            dma("G", hsem, h_dram[oc, :, th * 1024:(th + 1) * 1024], hb[:])
                            hstore[hs] = n[hsem]
                            if gu_mode < 6:
                                continue
                            wait("A", "G")
                            op("A", lambda e, hb=hb: e.activation(out=habs[:], in_=hb[:],
                                                                  func=AF.Abs, bias=0.0, scale=1.0), tick="A")
                            habs_done[hs] = n["A"]
                            wait("V", "A", habs_done[hs])
                            op("V", lambda e, th=th: e.tensor_tensor(
                                mxt[:, th * 1024:(th + 1) * 1024],
                                mxt[:, th * 1024:(th + 1) * 1024], habs[:], ALU.max), tick="V")

            def phase_hscale():
                for tt in range(NT):
                    tb = 6 + (tt % 2)
                    wait("T", "V", evac_cnt[tb])
                    op("T", lambda e, tt=tt, tb=tb: e.transpose(
                        pb[tb][:, 0:128], mxt[:, tt * 128:(tt + 1) * 128], idf[:]), tick="P")
                    wait("V", "P")
                    op("V", lambda e, tt=tt, tb=tb: e.tensor_reduce(
                        st["maxh"][:, tt:tt + 1], pb[tb][:, 0:128], AX.X, ALU.max), tick="V",
                       drain=(tt == NT - 1))
                    evac_cnt[tb] = n["V"]
                op("V", lambda e: e.tensor_scalar(st["t16a"][:], st["maxh"][:], mB[:, 1:2],
                                                  EPS, ALU.mult, ALU.max), drain=True)
                op("V", lambda e: e.reciprocal(st["t16b"][:], st["t16a"][:]), drain=True)
                op("V", lambda e: e.tensor_scalar(st["sqh"][:], st["t16b"][:], mB[:, 1:2],
                                                  127.0, ALU.mult, ALU.mult), drain=True)
                op("V", lambda e: e.tensor_scalar(st["cdown"][:], st["t16a"][:], mB[:, 2:3],
                                                  -0.5 / 127.0, ALU.mult, ALU.mult), tick="V", drain=True)

            def phase_down(call):
                if call == 0:
                    wd_pool = {0: 0, 1: 0}
                    for c in range(NHC):
                        sl = c % 2
                        dsem = "dSl0" if sl == 0 else "dSl1"
                        wait("S", "V", wd_pool[sl])
                        dma("S", dsem, wdstag[sl][:], wdT3[c])
                        wait("V", dsem)
                        op("V", lambda e, sl=sl: e.tensor_scalar(
                            wdscr[:], wdstag[sl][:], mBhn[:, 2:3], None, ALU.is_lt))
                        op("V", lambda e, sl=sl: e.tensor_scalar(
                            wdstag[sl][:], wdstag[sl][:], mBh[:, 2:3], None, ALU.is_gt))
                        op("V", lambda e, sl=sl, c=c: e.tensor_tensor(
                            terndn[:, c * 1024:(c + 1) * 1024], wdstag[sl][:],
                            wdscr[:], ALU.subtract), tick="V")
                        wd_pool[sl] = n["V"]
                        wait("G", "V", wd_pool[sl])
                        dma("G", "dG", tern_d[c], terndn[:, c * 1024:(c + 1) * 1024])
                    wd_all_V = n["V"]
                else:
                    terndn3 = terndn[:, :].rearrange("p (c o) -> p c o", o=1024)
                    for g8 in range(4):
                        dma("S", "dS", terndn3[:, g8 * 8:(g8 + 1) * 8, :],
                            tern_d3[:, g8 * 8:(g8 + 1) * 8, :])
                    wd_all_V = n["V"]
                terndn_dS = n["dS"]
                pool_u = {0: 0, 1: 0}
                vch_done = {0: 0, 1: 0}
                evac_all = 0
                prev_down_p = 0
                for tq in range(4):
                    for i in range(4):
                        tt = tq * 4 + i
                        tb = 6 + (i % 2)
                        wait("T", "V", evac_cnt[tb])
                        op("T", lambda e, tt=tt, tb=tb: e.transpose(
                            pb[tb][0:1, 0:128], st["sqh"][:, tt:tt + 1], idf[:]), tick="P")
                        wait("V", "P")
                        op("V", lambda e, i=i, tb=tb: e.tensor_copy(
                            srow[0:1, i * 128:(i + 1) * 128], pb[tb][0:1, 0:128]), tick="V")
                        evac_cnt[tb] = n["V"]
                    wait("T", "V", evac_cnt[6])
                    op("T", lambda e: e.matmul(pb[6][:, 0:512], onesf[0:1, :], srow[0:1, 0:512],
                                               start=True, stop=True), tick="P")
                    wait("V", "P")
                    op("V", lambda e: e.tensor_copy(sBq[:], pb[6][:, 0:512]), tick="V")
                    evac_cnt[6] = n["V"]
                    sbq_ready = n["V"]
                    vch_tick = {}
                    for hc in range(NHC):
                        sl = hc % 2
                        hsem = "dH0" if sl == 0 else "dH1"
                        heng = "S" if sl == 0 else "A"
                        wait(heng, "G", pool_u[sl])
                        dma(heng, hsem, hrb[sl][:], h_dram[hc, :, tq * 512:(tq + 1) * 512])
                        wait("G", hsem)
                        wait("G", "V", sbq_ready)
                        wait("G", "V", vch_done[sl])
                        op("G", lambda e, sl=sl: e.tensor_tensor(uq5[sl][:], hrb[sl][:], sBq[:], ALU.mult), tick="G")
                        pool_u[sl] = n["G"]
                        wait("V", "G", pool_u[sl])
                        wait("V", "P", prev_down_p)
                        op("V", lambda e, sl=sl: e.tensor_copy(k5i[:], uq5[sl][:]))
                        op("V", lambda e, sl=sl: e.tensor_copy(kf5[sl][:], k5i[:]))
                        op("V", lambda e, sl=sl: e.tensor_tensor(v5[sl][:], uq5[sl][:], kf5[sl][:], ALU.subtract))
                        op("V", lambda e, sl=sl: e.scalar_tensor_tensor(
                            t15[sl][:], v5[sl][:], 0.5, kf5[sl][:], ALU.is_gt, ALU.add))
                        op("V", lambda e, sl=sl, hc=hc: e.scalar_tensor_tensor(
                            hq[:, hc * 512:(hc + 1) * 512], v5[sl][:], -0.5, t15[sl][:],
                            ALU.is_lt, ALU.subtract), tick="V")
                        vch_done[sl] = n["V"]
                        vch_tick[hc] = n["V"]
                    if call == 1:
                        wait("T", "dS", terndn_dS)
                    else:
                        wait("T", "V", wd_all_V)
                    wait("T", "V", evac_all)
                    for K in range(NHC):
                        wait("T", "V", vch_tick[K])
                        for i in range(4):
                            lhs = hq[:, K * 512 + i * 128: K * 512 + (i + 1) * 128]
                            op("T", lambda e, lhs=lhs, K=K, i=i: e.matmul(
                                pb[2 * i][:], lhs, terndn[:, K * 1024: K * 1024 + 512],
                                start=(K == 0), stop=(K == NHC - 1)))
                            op("T", lambda e, lhs=lhs, K=K, i=i: e.matmul(
                                pb[2 * i + 1][:], lhs, terndn[:, K * 1024 + 512: (K + 1) * 1024],
                                start=(K == 0), stop=(K == NHC - 1)),
                               tick="P" if (K == NHC - 1 and i == 3) else None)
                    prev_down_p = n["P"]
                    wait("V", "P", prev_down_p)
                    for i in range(4):
                        tt = tq * 4 + i
                        for j in range(2):
                            b = 2 * i + j
                            xsl = x_sb[:, tt * 1024 + j * 512: tt * 1024 + (j + 1) * 512]
                            op("V", lambda e, b=b, xsl=xsl, tt=tt: e.scalar_tensor_tensor(
                                xsl, pb[b][:], st["cdown"][:, tt:tt + 1], xsl, ALU.mult, ALU.add),
                               tick="V")
                    evac_all = n["V"]
                    for b in range(8):
                        evac_cnt[b] = n["V"]

            def ln_stats(dest):
                for tt in range(NT):
                    xt = x_sb[:, tt * 1024:(tt + 1) * 1024]
                    op("A", lambda e, xt=xt, tt=tt: e.activation(
                        out=dest[:], in_=xt, func=AF.Identity, bias=0.0,
                        scale=1.0, accum_out=st["macc2"][:, tt:tt + 1]), drain=True)
                tick("A", "A")
                wait("V", "A")
                op("V", lambda e: e.tensor_single_scalar(st["mu_neg"][:], st["macc2"][:], -1.0 / D, ALU.mult), tick="V", drain=True)
                wait("A", "V")
                for tt in range(NT):
                    xt = x_sb[:, tt * 1024:(tt + 1) * 1024]
                    op("A", lambda e, xt=xt, tt=tt: e.activation(
                        out=dest[:], in_=xt, func=AF.Square,
                        bias=st["mu_neg"][:, tt:tt + 1], scale=1.0,
                        accum_out=st["vacc"][:, tt:tt + 1]), drain=True)
                tick("A", "A")
                wait("V", "A")
                op("V", lambda e: e.tensor_single_scalar(st["t16a"][:], st["vacc"][:], 1.0 / D, ALU.mult), tick="V", drain=True)
                wait("A", "V")
                op("A", lambda e: e.activation(out=st["t16b"][:], in_=st["t16a"][:],
                                               func=AF.Sqrt, bias=epsc[:], scale=1.0), tick="A", drain=True)
                wait("V", "A")
                op("V", lambda e: e.reciprocal(st["inv"][:], st["t16b"][:]), drain=True)
                op("V", lambda e: e.tensor_tensor(st["cB"][:], st["mu_neg"][:], st["inv"][:], ALU.mult), tick="V", drain=True)

            def phase_kv():
                ln_stats(xlb)
                stats_v = n["V"]
                kvsl3 = kvslab[:, :].rearrange("p (c o) -> p c o", o=2048)
                for h2 in range(2):
                    dma("S", "dS", kvsl3[:, h2 * 4:(h2 + 1) * 4, :],
                        wkvT4[:, h2 * 4:(h2 + 1) * 4, :])
                dma("S", "dS", bkv_sb[:], bkvB)
                slab_dS = n["dS"]
                kv_evac_last = 0
                kv_evac = {0: 0, 1: 0}
                cmm = {0: 0, 1: 0}
                xl_cons = 0
                xlT_ev = {0: 0, 1: 0}
                kvmm_p = {0: 0, 1: 0}
                for tt in range(NT):
                    sl = tt % 2
                    if tt == 0:
                        wait("A", "V", stats_v)
                    wait("A", "P", xl_cons)
                    op("A", lambda e, tt=tt: e.activation(
                        out=xlb[:], in_=x_sb[:, tt * 1024:(tt + 1) * 1024], func=AF.Identity,
                        bias=st["cB"][:, tt:tt + 1], scale=st["inv"][:, tt:tt + 1]), tick="A")
                    wait("T", "A")
                    for dc in range(ND):
                        tb = 4 + (dc % 2)
                        wait("T", "V", evac_cnt[tb])
                        op("T", lambda e, dc=dc, tb=tb: e.transpose(
                            pb[tb][:, 0:128], xlb[:, dc * 128:(dc + 1) * 128], idf[:]), tick="P")
                        if dc == ND - 1:
                            xl_cons = n["P"]
                        wait("V", "P")
                        wait("V", "P", kvmm_p[sl])
                        op("V", lambda e, dc=dc, sl=sl, tb=tb: e.tensor_copy(
                            xlT[sl][:, dc * 128:(dc + 1) * 128], pb[tb][:, 0:128]), tick="V")
                        evac_cnt[tb] = n["V"]
                    xlT_ev[sl] = n["V"]
                    if tt == 0:
                        wait("T", "dS", slab_dS)
                    wait("T", "V", xlT_ev[sl])
                    wait("T", "V", kv_evac_last)
                    for K in range(ND):
                        for b in range(4):
                            op("T", lambda e, K=K, b=b, sl=sl: e.matmul(
                                pb[b][:], xlT[sl][:, K * 128:(K + 1) * 128],
                                kvslab[:, K * 2048 + b * 512: K * 2048 + (b + 1) * 512],
                                start=(K == 0), stop=(K == ND - 1)),
                               tick="P" if (K == ND - 1 and b == 3) else None)
                    kvmm_p[sl] = n["P"]
                    wait("V", "P", kvmm_p[sl])
                    wait("V", "P", cmm[sl])
                    for b in range(4):
                        op("V", lambda e, b=b, sl=sl: e.tensor_tensor(
                            kvt[sl][:, b * 512:(b + 1) * 512], pb[b][:],
                            bkv_sb[:, b * 512:(b + 1) * 512], ALU.add), tick="V")
                    kv_evac[sl] = n["V"]
                    kv_evac_last = n["V"]
                    wait("T", "V", kv_evac_last)
                    for h in range(HEADS):
                        cb = 6 + h // 4
                        cs = (h % 4) * 128
                        op("T", lambda e, h=h, sl=sl, cb=cb, cs=cs, tt=tt: e.matmul(
                            pb[cb][:, cs:cs + 128],
                            kvt[sl][:, h * 128:(h + 1) * 128],
                            kvt[sl][:, 1024 + h * 128: 1024 + (h + 1) * 128],
                            start=(tt == 0 and h % 4 == 0),
                            stop=(tt == NT - 1 and h % 4 == 3)),
                           tick="P" if h == HEADS - 1 else None)
                    cmm[sl] = n["P"]
                wait("V", "P")
                op("V", lambda e: e.tensor_copy(cpart[:, 0:512], pb[6][:]))
                op("V", lambda e: e.tensor_copy(cpart[:, 512:1024], pb[7][:]), tick="V")
                evac_cnt[6] = n["V"]
                evac_cnt[7] = n["V"]
                wait("G", "V")
                dma("G", "dG", cc_in, cpart[:])
                wait("G", "dG")
                if role == "G":
                    bump("CC", 1, E.collective_compute(
                        "AllReduce", ALU.add,
                        replica_groups=[[0, 1, 2, 3], [4, 5, 6, 7]],
                        ins=[cc_in], outs=[cc_out]))
                else:
                    bump("CC", 1)
                wait("G", "CC")
                dma("G", "dG", csb[:], cc_out)

            def phase_q():
                wq3 = wq_sb[:, :].rearrange("p (c o) -> p c o", o=1024)
                wo3 = wo_sb[:, :].rearrange("p (c o) -> p c o", o=1024)
                dma("S", "dS", wq3, wqT4)
                dma("S", "dS", wo3, woT4)
                dma("S", "dS", bq_sb[:], bqB)
                dma("S", "dS", bo_sb[:], boB)
                wdma = n["dS"]
                wait("G", "dG")
                tick("G", "G")
                csb_g = n["G"]
                q_evac = {0: 0, 1: 0}
                q_evac_last = 0
                att_evac = 0
                out_evac = 0
                xl_cons = 0
                xlT_ev = {0: 0, 1: 0}
                qmm_p = {0: 0, 1: 0}
                qtrans_p = {0: 0, 1: 0}
                womm_p = {0: 0, 1: 0}
                qt_done = {0: 0, 1: 0}
                for tt in range(NT):
                    sl = tt % 2
                    wait("A", "P", xl_cons)
                    wait("A", "V", out_evac)
                    op("A", lambda e, tt=tt: e.activation(
                        out=xlb7[:], in_=x_sb[:, tt * 1024:(tt + 1) * 1024], func=AF.Identity,
                        bias=st["cB"][:, tt:tt + 1], scale=st["inv"][:, tt:tt + 1]), tick="A")
                    wait("T", "A")
                    for dc in range(ND):
                        tb = 2 + (dc % 2)
                        wait("T", "V", evac_cnt[tb])
                        op("T", lambda e, dc=dc, tb=tb: e.transpose(
                            pb[tb][:, 0:128], xlb7[:, dc * 128:(dc + 1) * 128], idf[:]), tick="P")
                        if dc == ND - 1:
                            xl_cons = n["P"]
                        wait("V", "P")
                        wait("V", "P", qmm_p[sl])
                        op("V", lambda e, dc=dc, sl=sl, tb=tb: e.tensor_copy(
                            xlT7[sl][:, dc * 128:(dc + 1) * 128], pb[tb][:, 0:128]), tick="V")
                        evac_cnt[tb] = n["V"]
                    xlT_ev[sl] = n["V"]
                    if tt == 0:
                        wait("T", "dS", wdma)
                    wait("T", "V", xlT_ev[sl])
                    wait("T", "V", q_evac_last)
                    for K in range(ND):
                        for b in range(2):
                            op("T", lambda e, K=K, b=b, sl=sl: e.matmul(
                                pb[b][:], xlT7[sl][:, K * 128:(K + 1) * 128],
                                wq_sb[:, K * 1024 + b * 512: K * 1024 + (b + 1) * 512],
                                start=(K == 0), stop=(K == ND - 1)),
                               tick="P" if (K == ND - 1 and b == 1) else None)
                    qmm_p[sl] = n["P"]
                    wait("V", "P", qmm_p[sl])
                    wait("V", "P", qtrans_p[sl])
                    for b in range(2):
                        op("V", lambda e, b=b, sl=sl: e.tensor_tensor(
                            qtb[sl][:, b * 512:(b + 1) * 512], pb[b][:],
                            bq_sb[:, b * 512:(b + 1) * 512], ALU.add), tick="V")
                    q_evac[sl] = n["V"]
                    q_evac_last = n["V"]
                    wait("T", "V", q_evac_last)
                    qT = [qT_sb, qT_sb2][sl]
                    for h in range(HEADS):
                        tb = 2 + (h % 2)
                        wait("T", "V", evac_cnt[tb])
                        op("T", lambda e, h=h, sl=sl, tb=tb: e.transpose(
                            pb[tb][:, 0:128], qtb[sl][:, h * 128:(h + 1) * 128], idf[:]), tick="P")
                        wait("V", "P")
                        wait("V", "P", qt_done[sl])
                        op("V", lambda e, h=h, tb=tb, qT=qT: e.tensor_copy(
                            qT[:, h * 128:(h + 1) * 128], pb[tb][:, 0:128]), tick="V")
                        evac_cnt[tb] = n["V"]
                    qtrans_p[sl] = n["P"]
                    qtev = n["V"]
                    if tt == 0:
                        wait("T", "G", csb_g)
                    wait("T", "V", qtev)
                    wait("T", "V", att_evac)
                    for h in range(HEADS):
                        cb = 4 + h // 4
                        cs = (h % 4) * 128
                        op("T", lambda e, h=h, cb=cb, cs=cs, qT=qT: e.matmul(
                            pb[cb][:, cs:cs + 128], csb[:, h * 128:(h + 1) * 128],
                            qT[:, h * 128:(h + 1) * 128],
                            start=(h % 4 == 0), stop=(h % 4 == 3)),
                           tick="P" if h == HEADS - 1 else None)
                    qt_done[sl] = n["P"]
                    wait("V", "P", qt_done[sl])
                    wait("V", "P", womm_p[sl])
                    op("V", lambda e, sl=sl: e.tensor_copy(attT[sl][:, 0:512], pb[4][:]))
                    op("V", lambda e, sl=sl: e.tensor_copy(attT[sl][:, 512:1024], pb[5][:]), tick="V")
                    att_evac = n["V"]
                    wait("T", "V", att_evac)
                    wait("T", "V", out_evac)
                    for K in range(ND):
                        for b in range(2):
                            op("T", lambda e, K=K, b=b, sl=sl: e.matmul(
                                pb[6 + b][:], attT[sl][:, K * 128:(K + 1) * 128],
                                wo_sb[:, K * 1024 + b * 512: K * 1024 + (b + 1) * 512],
                                start=(K == 0), stop=(K == ND - 1)),
                               tick="P" if (K == ND - 1 and b == 1) else None)
                    womm = n["P"]
                    womm_p[sl] = womm
                    wait("V", "P", womm)
                    op("V", lambda e, tt=tt: e.tensor_tensor(
                        x_sb[:, tt * 1024:(tt + 1) * 1024],
                        x_sb[:, tt * 1024:(tt + 1) * 1024], bo_sb[:], ALU.add))
                    for b in range(2):
                        xsl = x_sb[:, tt * 1024 + b * 512: tt * 1024 + (b + 1) * 512]
                        op("V", lambda e, b=b, xsl=xsl: e.tensor_tensor(
                            xsl, xsl, pb[6 + b][:], ALU.add), tick="V")
                    out_evac = n["V"]

            steps = [phase_quant_x, lambda: phase_gateup(0), phase_hscale,
                     lambda: phase_down(0), phase_kv, phase_q,
                     phase_quant_x, lambda: phase_gateup(1), phase_hscale,
                     lambda: phase_down(1)]
            for si, step in enumerate(steps):
                if si < phases:
                    step()
                    barrier()
            for q4 in range(4):
                dma("S", "dS", y3[:, q4 * 4:(q4 + 1) * 4, :],
                    x_sb3[:, q4 * 4:(q4 + 1) * 4, :])
            if dbg:
                dma("S", "dS", o_tg, tern_g[0])
                dma("S", "dS", o_h, h_dram[0])
                dma("S", "dS", o_xq, xqT[:, 0:2048])
                dma("S", "dS", o_mb[:, 0:4], mB[:])
                dma("S", "dS", o_mb[:, 4:8], mBh[:])
                dma("S", "dS", o_mb[:, 8:12], mBhn[:])
                dma("S", "dS", o_st[:, 0:16], st["sqh"][:])
                dma("S", "dS", o_st[:, 16:32], st["maxh"][:])
                dma("S", "dS", o_st[:, 32:48], st["qv"][:])
                dma("S", "dS", o_st[:, 48:64], st["cdown"][:])
                dma("S", "dS", o_sx, sxB[:])
                dma("S", "dS", o_qr, qrow[0:1, :])
            wait("S", "dS")
            barrier()

        @block.sync
        def _(e):
            script("S", e)

        @block.gpsimd
        def _(e):
            script("G", e)

        @block.tensor
        def _(e):
            script("T", e)

        @block.vector
        def _(e):
            script("V", e)

        @block.scalar
        def _(e):
            script("A", e)

    return nc


def _host_inputs(L0_attn_x, L0_attn_w_qkv, L0_attn_b_qkv, L0_attn_w_o, L0_attn_b_o,
                 L0_ffn_w_gate, L0_ffn_w_up, L0_ffn_w_down):
    x = np.asarray(L0_attn_x, dtype=np.float32)
    wqkv = np.asarray(L0_attn_w_qkv, dtype=np.float32)
    bqkv = np.asarray(L0_attn_b_qkv, dtype=np.float32)
    wo = np.asarray(L0_attn_w_o, dtype=np.float32)
    bo = np.asarray(L0_attn_b_o, dtype=np.float32)
    wg = np.asarray(L0_ffn_w_gate, dtype=np.float32)
    wu = np.asarray(L0_ffn_w_up, dtype=np.float32)
    wd = np.asarray(L0_ffn_w_down, dtype=np.float32)

    c_, h_, dd_ = np.meshgrid(np.arange(3), np.arange(8), np.arange(128), indexing="ij")
    perm = (c_ * 1024 + dd_ * 8 + h_).reshape(-1)
    wqkvT = np.ascontiguousarray(wqkv.T[:, perm])
    bperm = bqkv[perm]

    common = {
        "wkvT": np.ascontiguousarray(wqkvT[:, 1024:3072]),
        "wqT": np.ascontiguousarray(wqkvT[:, 0:1024]),
        "woT": np.ascontiguousarray(wo.T),
        "wgT": np.ascontiguousarray(
            wg.T.reshape(8, 128, 8, 512).transpose(2, 1, 0, 3).reshape(8, 128, 4096)),
        "wuT": np.ascontiguousarray(
            wu.T.reshape(8, 128, 8, 512).transpose(2, 1, 0, 3).reshape(8, 128, 4096)),
        "wdT": np.ascontiguousarray(wd.T),
        "bkvB": np.ascontiguousarray(np.broadcast_to(bperm[1024:3072], (128, 2048))),
        "bqB": np.ascontiguousarray(np.broadcast_to(bperm[0:1024], (128, 1024))),
        "boB": np.ascontiguousarray(np.broadcast_to(bo, (128, 1024))),
        "idf_in": np.eye(128, dtype=np.float32),
        "ones_in": np.ones((128, 128), dtype=np.float32),
        "eps_in": np.full((128, 1), 1e-5, dtype=np.float32),
    }
    in_maps = []
    for core in range(8):
        b, tc = core // 4, core % 4
        m = dict(common)
        m["x_in"] = np.ascontiguousarray(x[b, tc * TPC:(tc + 1) * TPC, :])
        in_maps.append(m)
    return in_maps


def kernel(**inputs):
    in_maps = _host_inputs(**inputs)
    nc = build_nc()
    res = run_bass_kernel_spmd(nc, in_maps, core_ids=list(range(8)))
    out = np.empty((2, 8192, 1024), dtype=np.float32)
    for core in range(8):
        b, tc = core // 4, core % 4
        out[b, tc * TPC:(tc + 1) * TPC, :] = res.results[core]["y_out"]
    return out


# version 14 (final)
# speedup vs baseline: 1.0357x; 1.0051x over previous
import contextlib
import numpy as np
import concourse.bass as bass
import concourse.mybir as mybir
from concourse.bass_utils import run_bass_kernel_spmd

F32 = mybir.dt.float32
BF16 = mybir.dt.bfloat16
I8 = mybir.dt.int8
I32 = mybir.dt.int32
AF = mybir.ActivationFunctionType
ALU = mybir.AluOpType
AX = mybir.AxisListType

TPC = 2048
NT = 16
D = 1024
ND = 8
NI = 4096
NHC = 32
NOB = 8
HEADS = 8
EPS = 1e-5
NW = float(NI * D)

ENGINES = ("S", "G", "T", "V", "A")
SEMS = ("dS", "dSl0", "dSl1", "dH0", "dH1", "dG", "dC0", "dC1", "dGh0", "dGh1",
        "P", "V", "A", "G", "CC", "bS", "bG", "bT", "bV", "bA")

OVB = 102400


def build_nc(phases=12, gu_limit=99, gu_mode=6, dbg=False, q_serial=False):
    nc = bass.Bass("TRN2", target_bir_lowering=False, debug=False,
                   detect_race_conditions=False)

    x_in = nc.dram_tensor("x_in", [TPC, D], F32, kind="ExternalInput").ap()
    wkvT = nc.dram_tensor("wkvT", [D, 2048], F32, kind="ExternalInput").ap()
    wqT_d = nc.dram_tensor("wqT", [D, 1024], F32, kind="ExternalInput").ap()
    woT_d = nc.dram_tensor("woT", [D, 1024], F32, kind="ExternalInput").ap()
    wgT = nc.dram_tensor("wgT", [NOB, 128, 4096], F32, kind="ExternalInput").ap()
    wuT = nc.dram_tensor("wuT", [NOB, 128, 4096], F32, kind="ExternalInput").ap()
    wdT = nc.dram_tensor("wdT", [NI, D], F32, kind="ExternalInput").ap()
    bkvB = nc.dram_tensor("bkvB", [128, 2048], F32, kind="ExternalInput").ap()
    bqB = nc.dram_tensor("bqB", [128, 1024], F32, kind="ExternalInput").ap()
    boB = nc.dram_tensor("boB", [128, 1024], F32, kind="ExternalInput").ap()
    idf_in = nc.dram_tensor("idf_in", [128, 128], F32, kind="ExternalInput").ap()
    ones_in = nc.dram_tensor("ones_in", [128, 128], F32, kind="ExternalInput").ap()
    eps_in = nc.dram_tensor("eps_in", [128, 1], F32, kind="ExternalInput").ap()
    y_out = nc.dram_tensor("y_out", [TPC, D], F32, kind="ExternalOutput").ap()
    if dbg:
        o_tg = nc.dram_tensor("o_tg", [128, 4096], BF16, kind="ExternalOutput").ap()
        o_h = nc.dram_tensor("o_h", [128, 2048], F32, kind="ExternalOutput").ap()
        o_xq = nc.dram_tensor("o_xq", [128, 2048], BF16, kind="ExternalOutput").ap()
        o_mb = nc.dram_tensor("o_mb", [128, 12], F32, kind="ExternalOutput").ap()
        o_st = nc.dram_tensor("o_st", [128, 64], F32, kind="ExternalOutput").ap()
        o_sx = nc.dram_tensor("o_sx", [128, 2048], F32, kind="ExternalOutput").ap()
        o_qr = nc.dram_tensor("o_qr", [1, 2048], F32, kind="ExternalOutput").ap()

    h_dram = nc.dram_tensor("h_dram", [NHC, 128, TPC], F32).ap()
    tern_g = nc.dram_tensor("tern_g", [NOB, 128, 4096], BF16).ap()
    tern_u = nc.dram_tensor("tern_u", [NOB, 128, 4096], BF16).ap()
    tern_d = nc.dram_tensor("tern_d", [NHC, 128, 1024], BF16).ap()
    cc_in = nc.dram_tensor("cc_in", [128, 1024], F32).ap()
    cc_out = nc.dram_tensor("cc_out", [128, 1024], F32).ap()

    x_sb = nc.alloc_sbuf_tensor("x_sb", [128, NT * 1024], F32)
    idf = nc.alloc_sbuf_tensor("idf", [128, 128], F32)
    onesf = nc.alloc_sbuf_tensor("onesf", [128, 128], F32)
    epsc = nc.alloc_sbuf_tensor("epsc", [128, 1], F32)
    csb = nc.alloc_sbuf_tensor("csb", [128, 1024], F32)
    cpart = nc.alloc_sbuf_tensor("cpart", [128, 1024], F32)
    st_names = ["macc2", "vacc", "mu_neg", "inv", "rmx", "rmn", "c1n", "cB",
                "qv", "maxh", "cdown", "sqh", "t16a", "t16b"]
    st = {nm: nc.alloc_sbuf_tensor("st_" + nm, [128, 16], F32) for nm in st_names}
    macc = nc.alloc_sbuf_tensor("macc", [128, 64], F32)
    mrow = nc.alloc_sbuf_tensor("mrow", [128, 4], F32)
    msc2 = nc.alloc_sbuf_tensor("msc2", [1, 4], F32)
    mB = nc.alloc_sbuf_tensor("mB", [128, 4], F32)
    mBh = nc.alloc_sbuf_tensor("mBh", [128, 4], F32)
    mBhn = nc.alloc_sbuf_tensor("mBhn", [128, 4], F32)

    def ov(name, shape, dtype, off):
        assert off >= 0
        return nc.alloc_sbuf_tensor_at(name, shape, dtype, offset=OVB + off)

    xqT = ov("xqT", [128, ND * TPC], BF16, 0)
    uq = [ov(f"uq{i}", [128, 1024], F32, 32768 + i * 4096) for i in range(2)]
    kfb = [ov(f"kfb{i}", [128, 1024], F32, 40960 + i * 4096) for i in range(2)]
    kqf = [ov(f"kqf{i}", [128, 1024], F32, 49152 + i * 4096) for i in range(2)]
    vres = [ov(f"vres{i}", [128, 1024], F32, 57344 + i * 4096) for i in range(2)]
    k8b = [ov(f"k8b{i}", [128, 1024], I8, 65536 + i * 1024) for i in range(2)]
    scr = ov("scr", [128, 2048], F32, 67584)
    xqi = ov("xqi", [128, 1024], I32, 67584)
    stag = [ov(f"stag{i}", [128, 2048], F32, 75776 + i * 8192) for i in range(2)]
    terng_sb = ov("terng_sb", [128, 4096], BF16, 92160)
    ternu_sb = ov("ternu_sb", [128, 4096], BF16, 100352)
    terng_sb2 = ov("terng_sb2", [128, 4096], BF16, 53248)
    ternu_sb2 = nc.alloc_sbuf_tensor_at("ternu_sb2", [128, 4096], BF16, offset=94208)
    sgb = ov("sgb", [128, 1024], F32, 61440)
    g2b = ov("g2b", [128, 1024], F32, 108544)
    u2b = ov("u2b", [128, 1024], F32, 112640)
    hbb = [ov(f"hbb{i}", [128, 1024], F32, 116736 + i * 4096) for i in range(2)]
    habs = ov("habs", [128, 1024], F32, 32768)
    sxB = ov("sxB", [128, 2048], F32, 36864)
    mxt = ov("mxt", [128, 2048], F32, 45056)
    qrow = ov("qrow", [1, 2048], F32, 53248)
    terndn = ov("terndn", [128, NHC * 1024], BF16, 0)
    wdstag = [ov(f"wdstag{i}", [128, 1024], F32, 65536 + i * 4096) for i in range(2)]
    wdscr = ov("wdscr", [128, 1024], F32, 73728)
    hq = ov("hq", [128, NHC * 512], BF16, 65536)
    hrb = [ov(f"hrb{i}", [128, 512], F32, 98304 + i * 2048) for i in range(2)]
    uq5 = [ov(f"uq5{i}", [128, 512], F32, 102400 + i * 2048) for i in range(2)]
    kf5 = [ov(f"kf5{i}", [128, 512], F32, 106496 + i * 2048) for i in range(2)]
    k85 = [ov(f"k85{i}", [128, 512], I8, 110592 + i * 512) for i in range(2)]
    v5 = [ov(f"v5{i}", [128, 512], F32, 111616 + i * 2048) for i in range(2)]
    t15 = [ov(f"t15{i}", [128, 512], F32, 115712 + i * 2048) for i in range(2)]
    sBq = ov("sBq", [128, 512], F32, 119808)
    srow = ov("srow", [1, 512], F32, 121856)
    k5i = ov("k5i", [128, 512], I32, 123904)
    kvslab = ov("kvslab", [128, ND * 2048], F32, 0)
    kvt = [ov(f"kvt{i}", [128, 2048], F32, 65536 + i * 8192) for i in range(2)]
    xlb = ov("xlb", [128, 1024], F32, 81920)
    xlT = [ov(f"xlT{i}", [128, 1024], F32, 86016 + i * 4096) for i in range(2)]
    bkv_sb = ov("bkv_sb", [128, 2048], F32, 94208)
    wq_sb = ov("wq_sb", [128, ND * 1024], F32, 0)
    wo_sb = ov("wo_sb", [128, ND * 1024], F32, 32768)
    qtb = [ov(f"qtb{i}", [128, 1024], F32, 65536 + i * 4096) for i in range(2)]
    qT_sb = ov("qT_sb", [128, 1024], F32, 73728)
    qT_sb2 = ov("qT_sb2", [128, 1024], F32, 106496)
    attT = [ov(f"attT{i}", [128, 1024], F32, 77824 + i * 4096) for i in range(2)]
    xlb7 = ov("xlb7", [128, 1024], F32, 86016)
    xlT7 = [ov(f"xlT7{i}", [128, 1024], F32, 90112 + i * 4096) for i in range(2)]
    bq_sb = ov("bq_sb", [128, 1024], F32, 98304)
    bo_sb = ov("bo_sb", [128, 1024], F32, 102400)

    pb = [nc.alloc_psum_tensor(f"pb{i}", [128, 512], F32) for i in range(8)]

    x_in3 = x_in.rearrange("(t p) d -> p t d", p=128)
    y3 = y_out.rearrange("(t p) d -> p t d", p=128)
    wdT3 = wdT.rearrange("(c p) o -> c p o", p=128)
    wkvT4 = wkvT.rearrange("(c p) o -> p c o", p=128)
    wqT4 = wqT_d.rearrange("(c p) o -> p c o", p=128)
    woT4 = woT_d.rearrange("(c p) o -> p c o", p=128)
    tern_d3 = tern_d.rearrange("c p o -> p c o")

    with contextlib.ExitStack() as stack:
        SEMH = {s: stack.enter_context(nc.semaphore(s)) for s in SEMS}
        block = stack.enter_context(nc.Block())

        def script(role, E):
            n = {s: 0 for s in SEMS}
            evac_cnt = {i: 0 for i in range(8)}

            def bump(sem, k, inst=None):
                n[sem] += k
                if inst is not None:
                    inst.then_inc(SEMH[sem], k)

            def wait(eng, sem, val=None):
                if role == eng:
                    E.wait_ge(SEMH[sem], n[sem] if val is None else val)

            def dma(eng, sem, out, in_):
                if role == eng:
                    bump(sem, 16, E.dma_start(out=out, in_=in_))
                else:
                    bump(sem, 16)

            def op(eng, fn, tick=None, drain=False):
                if role == eng:
                    inst = fn(E)
                    if tick is not None:
                        bump(tick, 1, inst)
                    if drain:
                        E.drain()
                elif tick is not None:
                    bump(tick, 1)

            def tick(eng, sem):
                if role == eng:
                    bump(sem, 1, E.nop())
                else:
                    bump(sem, 1)

            def barrier():
                if role == "S":
                    for s in ("dS", "dSl0", "dSl1", "dH0", "dH1"):
                        E.wait_ge(SEMH[s], n[s])
                if role == "G":
                    for s in ("dG", "dC0", "dC1", "dGh0", "dGh1", "CC"):
                        E.wait_ge(SEMH[s], n[s])
                own = {"T": "P", "V": "V", "A": "A", "G": "G"}
                if role in own:
                    E.wait_ge(SEMH[own[role]], n[own[role]])
                bar = {"S": "bS", "G": "bG", "T": "bT", "V": "bV", "A": "bA"}
                for eng in ENGINES:
                    if role == eng:
                        bump(bar[eng], 1, E.nop())
                    else:
                        bump(bar[eng], 1)
                for eng in ENGINES:
                    if eng != role:
                        E.wait_ge(SEMH[bar[eng]], n[bar[eng]])

            x_sb3 = x_sb[:, :].rearrange("p (t d) -> p t d", d=1024)
            for q4 in range(4):
                dma("S", "dS", x_sb3[:, q4 * 4:(q4 + 1) * 4, :],
                    x_in3[:, q4 * 4:(q4 + 1) * 4, :])
            dma("S", "dS", idf[:], idf_in)
            dma("S", "dS", onesf[:], ones_in)
            dma("S", "dS", epsc[:], eps_in)

            scan_a = {0: 0, 1: 0}
            pieces = []
            for c in range(NOB):
                pieces.append((wgT, c, 0, 2048, 0 + 2 * c))
                pieces.append((wgT, c, 2048, 2048, 1 + 2 * c))
            for c in range(NOB):
                pieces.append((wuT, c, 0, 2048, 16 + 2 * c))
                pieces.append((wuT, c, 2048, 2048, 17 + 2 * c))
            for c in range(NHC):
                pieces.append((wdT3, c, 0, 1024, 32 + c))
            for ci, (src, c, o0, w, col) in enumerate(pieces):
                sl = ci % 2
                dsem = "dSl0" if sl == 0 else "dSl1"
                wait("S", "A", scan_a[sl])
                dma("S", dsem, stag[sl][:, 0:w], src[c][:, o0:o0 + w])
                wait("A", dsem)
                op("A", lambda e, sl=sl, w=w, col=col: e.activation(
                    out=scr[:, 0:w], in_=stag[sl][:, 0:w], func=AF.Abs, bias=0.0,
                    scale=1.0, accum_out=macc[:, col:col + 1]), tick="A", drain=True)
                scan_a[sl] = n["A"]
            wait("V", "A")
            op("V", lambda e: e.tensor_reduce(mrow[:, 0:1], macc[:, 0:16], AX.X, ALU.add))
            op("V", lambda e: e.tensor_reduce(mrow[:, 1:2], macc[:, 16:32], AX.X, ALU.add))
            op("V", lambda e: e.tensor_reduce(mrow[:, 2:3], macc[:, 32:64], AX.X, ALU.add), tick="V", drain=True)
            wait("T", "dS", 112)
            wait("T", "V")
            op("T", lambda e: e.matmul(pb[0][0:1, 0:3], onesf[:, 0:1], mrow[:, 0:3],
                                       start=True, stop=True), tick="P")
            wait("V", "P")
            op("V", lambda e: e.tensor_scalar(msc2[0:1, 0:3], pb[0][0:1, 0:3],
                                              1.0 / NW, EPS, ALU.mult, ALU.max), tick="V", drain=True)
            wait("T", "V")
            op("T", lambda e: e.matmul(pb[1][:, 0:3], onesf[0:1, :], msc2[0:1, 0:3],
                                       start=True, stop=True), tick="P")
            wait("V", "P")
            op("V", lambda e: e.tensor_copy(mB[:, 0:3], pb[1][:, 0:3]), drain=True)
            op("V", lambda e: e.tensor_single_scalar(mBhn[:, 0:3], mB[:, 0:3], -0.5, ALU.mult), drain=True)
            op("V", lambda e: e.tensor_single_scalar(mBh[:, 0:3], mB[:, 0:3], 0.5, ALU.mult), tick="V", drain=True)

            def phase_quant_x():
                wait("A", "dS", 112)
                for tt in range(NT):
                    xt = x_sb[:, tt * 1024:(tt + 1) * 1024]
                    op("A", lambda e, xt=xt, tt=tt: e.activation(
                        out=scr[:, 0:1024], in_=xt, func=AF.Identity, bias=0.0,
                        scale=1.0, accum_out=st["macc2"][:, tt:tt + 1]), drain=True)
                tick("A", "A")
                wait("V", "A")
                op("V", lambda e: e.tensor_single_scalar(st["mu_neg"][:], st["macc2"][:], -1.0 / D, ALU.mult), tick="V", drain=True)
                wait("A", "V")
                for tt in range(NT):
                    xt = x_sb[:, tt * 1024:(tt + 1) * 1024]
                    op("A", lambda e, xt=xt, tt=tt: e.activation(
                        out=scr[:, 0:1024], in_=xt, func=AF.Square,
                        bias=st["mu_neg"][:, tt:tt + 1], scale=1.0,
                        accum_out=st["vacc"][:, tt:tt + 1]), drain=True)
                tick("A", "A")
                wait("V", "A")
                op("V", lambda e: e.tensor_single_scalar(st["t16a"][:], st["vacc"][:], 1.0 / D, ALU.mult), tick="V", drain=True)
                wait("A", "V")
                op("A", lambda e: e.activation(out=st["t16b"][:], in_=st["t16a"][:],
                                               func=AF.Sqrt, bias=epsc[:], scale=1.0), tick="A", drain=True)
                wait("V", "A")
                op("V", lambda e: e.reciprocal(st["inv"][:], st["t16b"][:]), drain=True)
                for tt in range(NT):
                    xt = x_sb[:, tt * 1024:(tt + 1) * 1024]
                    op("V", lambda e, xt=xt, tt=tt: e.tensor_reduce(st["rmx"][:, tt:tt + 1], xt, AX.X, ALU.max))
                    op("V", lambda e, xt=xt, tt=tt: e.tensor_reduce(st["rmn"][:, tt:tt + 1], xt, AX.X, ALU.min),
                       drain=(tt == NT - 1))
                op("V", lambda e: e.tensor_tensor(st["t16a"][:], st["rmx"][:], st["mu_neg"][:], ALU.add), drain=True)
                op("V", lambda e: e.tensor_tensor(st["t16b"][:], st["rmn"][:], st["mu_neg"][:], ALU.add), drain=True)
                op("V", lambda e: e.scalar_tensor_tensor(st["t16a"][:], st["t16b"][:], -1.0,
                                                         st["t16a"][:], ALU.mult, ALU.max), drain=True)
                op("V", lambda e: e.tensor_tensor(st["t16a"][:], st["t16a"][:], st["inv"][:], ALU.mult), drain=True)
                op("V", lambda e: e.tensor_scalar_max(st["t16a"][:], st["t16a"][:], EPS), drain=True)
                op("V", lambda e: e.tensor_single_scalar(st["qv"][:], st["t16a"][:], 1.0 / 127.0, ALU.mult), drain=True)
                op("V", lambda e: e.reciprocal(st["t16b"][:], st["t16a"][:]), drain=True)
                op("V", lambda e: e.tensor_tensor(st["t16b"][:], st["t16b"][:], st["inv"][:], ALU.mult), drain=True)
                op("V", lambda e: e.tensor_single_scalar(st["c1n"][:], st["t16b"][:], -127.0, ALU.mult), drain=True)
                op("V", lambda e: e.tensor_tensor(st["cB"][:], st["mu_neg"][:], st["c1n"][:], ALU.mult), tick="V", drain=True)
                stats_v = n["V"]
                wait("A", "V", stats_v)
                vq_done = {0: 0, 1: 0}
                tq_done = {0: 0, 1: 0}
                for tt in range(NT):
                    sl = tt % 2
                    if tt >= 2:
                        wait("A", "V", vq_done[sl])
                    xt = x_sb[:, tt * 1024:(tt + 1) * 1024]
                    op("A", lambda e, xt=xt, tt=tt, sl=sl: e.activation(
                        out=uq[sl][:], in_=xt, func=AF.Identity,
                        bias=st["cB"][:, tt:tt + 1], scale=st["c1n"][:, tt:tt + 1]), tick="A")
                    wait("V", "A")
                    wait("V", "P", tq_done[sl])
                    op("V", lambda e, sl=sl: e.tensor_copy(xqi[:], uq[sl][:]))
                    op("V", lambda e, sl=sl: e.tensor_copy(kfb[sl][:], xqi[:]))
                    op("V", lambda e, sl=sl: e.tensor_tensor(vres[sl][:], uq[sl][:], kfb[sl][:], ALU.subtract))
                    op("V", lambda e, sl=sl: e.scalar_tensor_tensor(kqf[sl][:], vres[sl][:], 0.5,
                                                                    kfb[sl][:], ALU.is_gt, ALU.add))
                    op("V", lambda e, sl=sl: e.scalar_tensor_tensor(kqf[sl][:], vres[sl][:], -0.5,
                                                                    kqf[sl][:], ALU.is_lt, ALU.subtract), tick="V")
                    vq_done[sl] = n["V"]
                    wait("T", "V", vq_done[sl])
                    for dc in range(ND):
                        tb = 6 + (dc % 2)
                        wait("T", "V", evac_cnt[tb])
                        op("T", lambda e, dc=dc, sl=sl, tb=tb: e.transpose(
                            pb[tb][:, 0:128], kqf[sl][:, dc * 128:(dc + 1) * 128], idf[:]), tick="P")
                        wait("V", "P")
                        op("V", lambda e, dc=dc, tt=tt, tb=tb: e.tensor_copy(
                            xqT[:, dc * TPC + tt * 128: dc * TPC + (tt + 1) * 128],
                            pb[tb][:, 0:128]), tick="V")
                        evac_cnt[tb] = n["V"]
                    tq_done[sl] = n["P"]
                    if q_serial:
                        barrier()

            def phase_gateup(call, gu_limit=gu_limit, gu_mode=gu_mode):
                op("G", lambda e: e.memset(mxt[:], 0.0), tick="G")
                mxt_zeroed = n["G"]
                wait("V", "G", mxt_zeroed)
                for tt in range(NT):
                    tb = 6 + (tt % 2)
                    wait("T", "V", evac_cnt[tb])
                    op("T", lambda e, tt=tt, tb=tb: e.transpose(
                        pb[tb][0:1, 0:128], st["qv"][:, tt:tt + 1], idf[:]), tick="P")
                    wait("V", "P")
                    op("V", lambda e, tt=tt, tb=tb: e.tensor_copy(
                        qrow[0:1, tt * 128:(tt + 1) * 128], pb[tb][0:1, 0:128]), tick="V")
                    evac_cnt[tb] = n["V"]
                for b in range(4):
                    tb = 6 + (b % 2)
                    wait("T", "V", evac_cnt[tb])
                    op("T", lambda e, b=b, tb=tb: e.matmul(
                        pb[tb][:, 0:512], onesf[0:1, :], qrow[0:1, b * 512:(b + 1) * 512],
                        start=True, stop=True), tick="P")
                    wait("V", "P")
                    op("V", lambda e, b=b, tb=tb: e.tensor_copy(
                        sxB[:, b * 512:(b + 1) * 512], pb[tb][:, 0:512]), tick="V")
                    evac_cnt[tb] = n["V"]
                qrow_cons = n["P"]
                if gu_limit == 0:
                    return
                stag3 = [s[:, :].rearrange("p (c o) -> p c o", o=512) for s in stag]
                tern_sbs = [[terng_sb, terng_sb2], [ternu_sb, ternu_sb2]]
                gu_slabdone = {}
                pool_tern = {}
                slab_ld = {}
                stag_free = {0: 0, 1: 0}
                evacV = {0: 0, 1: 0}
                habs_done = {0: 0, 1: 0}
                hstore = {0: 0, 1: 0}
                last_hmul_G = 0
                dg_store = {}
                nob = min(NOB, gu_limit)

                def emit_slab(ob):
                    for mi in range(2):
                        wsrc = [wgT, wuT][mi]
                        tdram = [tern_g, tern_u][mi]
                        tsb = tern_sbs[mi][ob % 2]
                        dsem = "dSl0" if mi == 0 else "dSl1"
                        if call == 0:
                            for half in range(2):
                                sl = mi
                                wait("S", "V", stag_free[sl])
                                dma("S", dsem, stag[sl][:],
                                    wsrc[ob][:, half * 2048:(half + 1) * 2048])
                                wait("V", dsem)
                                if half == 0:
                                    if ob == 1 and mi == 0:
                                        wait("V", "P", qrow_cons)
                                    if ob >= 2:
                                        wait("V", "P", gu_slabdone[(mi, ob - 2)])
                                        wait("V", "dG", dg_store[(mi, ob - 2)])
                                op("V", lambda e, sl=sl, mi=mi: e.tensor_scalar(
                                    scr[:], stag[sl][:], mBhn[:, mi:mi + 1], None, ALU.is_lt))
                                op("V", lambda e, sl=sl, mi=mi, tsb=tsb, half=half:
                                   e.scalar_tensor_tensor(
                                       tsb[:, half * 2048:(half + 1) * 2048], stag[sl][:],
                                       mBh[:, mi:mi + 1], scr[:], ALU.is_gt, ALU.subtract),
                                   tick="V")
                                stag_free[sl] = n["V"]
                            pool_tern[(mi, ob)] = n["V"]
                            wait("G", "V", pool_tern[(mi, ob)])
                            dma("G", "dG", tdram[ob], tsb[:])
                            dg_store[(mi, ob)] = n["dG"]
                        else:
                            if ob == 1 and mi == 0:
                                wait("S", "P", qrow_cons)
                            if ob >= 2:
                                wait("S", "P", gu_slabdone[(mi, ob - 2)])
                            dma("S", dsem, tsb[:], tdram[ob])
                            slab_ld[(mi, ob)] = n[dsem]

                emit_slab(0)
                for ob in range(nob):
                    if ob + 1 < nob:
                        emit_slab(ob + 1)
                    for ocl in range(4 if gu_mode >= 2 else 0):
                        oc = ob * 4 + ocl
                        for th in range(2):
                            hs = (oc * 2 + th) % 2
                            banks = [4 * hs + i for i in range(4)]
                            if call == 0:
                                wait("T", "V", pool_tern[(0, ob)])
                                wait("T", "V", pool_tern[(1, ob)])
                            else:
                                wait("T", "dSl0", slab_ld[(0, ob)])
                                wait("T", "dSl1", slab_ld[(1, ob)])
                            if gu_mode == 7:
                                wait("T", "V")
                            else:
                                wait("T", "V", max(evacV[hs],
                                                   *[evac_cnt[b] for b in banks]))
                            for mi in range(2):
                                tsb = tern_sbs[mi][ob % 2]
                                b0, b1 = banks[2 * mi], banks[2 * mi + 1]
                                for K in range(ND):
                                    lhs = tsb[:, K * 512 + ocl * 128: K * 512 + (ocl + 1) * 128]
                                    r0 = xqT[:, K * TPC + th * 1024: K * TPC + th * 1024 + 512]
                                    r1 = xqT[:, K * TPC + th * 1024 + 512: K * TPC + (th + 1) * 1024]
                                    op("T", lambda e, lhs=lhs, r0=r0, b0=b0, K=K: e.matmul(
                                        pb[b0][:], lhs, r0, start=(K == 0), stop=(K == ND - 1)))
                                    last = (mi == 1 and K == ND - 1)
                                    op("T", lambda e, lhs=lhs, r1=r1, b1=b1, K=K: e.matmul(
                                        pb[b1][:], lhs, r1, start=(K == 0), stop=(K == ND - 1)),
                                       tick="P" if last else None)
                            mmd = n["P"]
                            if ocl == 3 and th == 1:
                                gu_slabdone[(0, ob)] = mmd
                                gu_slabdone[(1, ob)] = mmd
                            if gu_mode < 3:
                                evacV[hs] = 0
                                continue
                            if gu_mode in (8, 9):
                                wait("V", "P", mmd)
                                for i2 in range(2):
                                    if gu_mode == 8:
                                        op("V", lambda e, b=banks[i2], i2=i2: e.tensor_copy(
                                            g2b[:, i2 * 512:(i2 + 1) * 512], pb[b][:]))
                                        op("V", lambda e, b=banks[2 + i2], i2=i2: e.tensor_copy(
                                            u2b[:, i2 * 512:(i2 + 1) * 512], pb[b][:]), tick="V")
                                    else:
                                        c0 = th * 1024 + i2 * 512
                                        op("V", lambda e, b=banks[i2], i2=i2, c0=c0: e.tensor_tensor(
                                            x_sb[:, i2 * 512:(i2 + 1) * 512], pb[b][:],
                                            sxB[:, c0:c0 + 512], ALU.mult))
                                        op("V", lambda e, b=banks[2 + i2], i2=i2, c0=c0: e.tensor_tensor(
                                            x_sb[:, 1024 + i2 * 512: 1024 + (i2 + 1) * 512], pb[b][:],
                                            sxB[:, c0:c0 + 512], ALU.mult), tick="V")
                                evacV[hs] = n["V"]
                                for b in banks:
                                    evac_cnt[b] = n["V"]
                                continue
                            wait("V", "P", mmd)
                            wait("V", "A", habs_done[hs])
                            for i2 in range(2):
                                c0 = th * 1024 + i2 * 512
                                op("V", lambda e, b=banks[i2], i2=i2, c0=c0: e.tensor_tensor(
                                    g2b[:, i2 * 512:(i2 + 1) * 512], pb[b][:],
                                    sxB[:, c0:c0 + 512], ALU.mult))
                            for i2 in range(2):
                                c0 = th * 1024 + i2 * 512
                                op("V", lambda e, b=banks[2 + i2], i2=i2, c0=c0: e.tensor_tensor(
                                    u2b[:, i2 * 512:(i2 + 1) * 512], pb[b][:],
                                    sxB[:, c0:c0 + 512], ALU.mult), tick="V")
                            evacV[hs] = n["V"]
                            for b in banks:
                                evac_cnt[b] = n["V"]
                            if gu_mode < 4 or gu_mode == 7:
                                continue
                            wait("A", "V", evacV[hs])
                            wait("A", "G", last_hmul_G)
                            op("A", lambda e: e.activation(out=sgb[:], in_=g2b[:], func=AF.Silu,
                                                           bias=0.0, scale=mB[:, 0:1]), tick="A")
                            if gu_mode < 5:
                                continue
                            wait("G", "A")
                            hsem = "dGh0" if hs == 0 else "dGh1"
                            wait("G", hsem, hstore[hs])
                            wait("G", "A", habs_done[hs])
                            hb = hbb[hs]
                            op("G", lambda e, hb=hb: e.tensor_tensor(hb[:], sgb[:], u2b[:], ALU.mult), tick="G")
                            last_hmul_G = n["G"]
                            dma("G", hsem, h_dram[oc, :, th * 1024:(th + 1) * 1024], hb[:])
                            hstore[hs] = n[hsem]
                            if gu_mode < 6:
                                continue
                            wait("A", "G")
                            op("A", lambda e, hb=hb: e.activation(out=habs[:], in_=hb[:],
                                                                  func=AF.Abs, bias=0.0, scale=1.0), tick="A")
                            habs_done[hs] = n["A"]
                            wait("V", "A", habs_done[hs])
                            op("V", lambda e, th=th: e.tensor_tensor(
                                mxt[:, th * 1024:(th + 1) * 1024],
                                mxt[:, th * 1024:(th + 1) * 1024], habs[:], ALU.max), tick="V")

            def phase_hscale():
                for tt in range(NT):
                    tb = 6 + (tt % 2)
                    wait("T", "V", evac_cnt[tb])
                    op("T", lambda e, tt=tt, tb=tb: e.transpose(
                        pb[tb][:, 0:128], mxt[:, tt * 128:(tt + 1) * 128], idf[:]), tick="P")
                    wait("V", "P")
                    op("V", lambda e, tt=tt, tb=tb: e.tensor_reduce(
                        st["maxh"][:, tt:tt + 1], pb[tb][:, 0:128], AX.X, ALU.max), tick="V",
                       drain=(tt == NT - 1))
                    evac_cnt[tb] = n["V"]
                op("V", lambda e: e.tensor_scalar(st["t16a"][:], st["maxh"][:], mB[:, 1:2],
                                                  EPS, ALU.mult, ALU.max), drain=True)
                op("V", lambda e: e.reciprocal(st["t16b"][:], st["t16a"][:]), drain=True)
                op("V", lambda e: e.tensor_scalar(st["sqh"][:], st["t16b"][:], mB[:, 1:2],
                                                  127.0, ALU.mult, ALU.mult), drain=True)
                op("V", lambda e: e.tensor_scalar(st["cdown"][:], st["t16a"][:], mB[:, 2:3],
                                                  -0.5 / 127.0, ALU.mult, ALU.mult), tick="V", drain=True)

            def phase_down(call):
                if call == 0:
                    wd_pool = {0: 0, 1: 0}
                    for c in range(NHC):
                        sl = c % 2
                        dsem = "dSl0" if sl == 0 else "dSl1"
                        wait("S", "V", wd_pool[sl])
                        dma("S", dsem, wdstag[sl][:], wdT3[c])
                        wait("V", dsem)
                        op("V", lambda e, sl=sl: e.tensor_scalar(
                            wdscr[:], wdstag[sl][:], mBhn[:, 2:3], None, ALU.is_lt))
                        op("V", lambda e, sl=sl, c=c: e.scalar_tensor_tensor(
                            terndn[:, c * 1024:(c + 1) * 1024], wdstag[sl][:],
                            mBh[:, 2:3], wdscr[:], ALU.is_gt, ALU.subtract), tick="V")
                        wd_pool[sl] = n["V"]
                        wait("G", "V", wd_pool[sl])
                        dma("G", "dG", tern_d[c], terndn[:, c * 1024:(c + 1) * 1024])
                    wd_all_V = n["V"]
                else:
                    terndn3 = terndn[:, :].rearrange("p (c o) -> p c o", o=1024)
                    for g8 in range(4):
                        dma("S", "dS", terndn3[:, g8 * 8:(g8 + 1) * 8, :],
                            tern_d3[:, g8 * 8:(g8 + 1) * 8, :])
                    wd_all_V = n["V"]
                terndn_dS = n["dS"]
                pool_u = {0: 0, 1: 0}
                vch_done = {0: 0, 1: 0}
                evac_all = 0
                prev_down_p = 0
                for tq in range(4):
                    for i in range(4):
                        tt = tq * 4 + i
                        tb = 6 + (i % 2)
                        wait("T", "V", evac_cnt[tb])
                        op("T", lambda e, tt=tt, tb=tb: e.transpose(
                            pb[tb][0:1, 0:128], st["sqh"][:, tt:tt + 1], idf[:]), tick="P")
                        wait("V", "P")
                        op("V", lambda e, i=i, tb=tb: e.tensor_copy(
                            srow[0:1, i * 128:(i + 1) * 128], pb[tb][0:1, 0:128]), tick="V")
                        evac_cnt[tb] = n["V"]
                    wait("T", "V", evac_cnt[6])
                    op("T", lambda e: e.matmul(pb[6][:, 0:512], onesf[0:1, :], srow[0:1, 0:512],
                                               start=True, stop=True), tick="P")
                    wait("V", "P")
                    op("V", lambda e: e.tensor_copy(sBq[:], pb[6][:, 0:512]), tick="V")
                    evac_cnt[6] = n["V"]
                    sbq_ready = n["V"]
                    vch_tick = {}
                    for hc in range(NHC):
                        sl = hc % 2
                        hsem = "dH0" if sl == 0 else "dH1"
                        heng = "S" if sl == 0 else "A"
                        wait(heng, "G", pool_u[sl])
                        dma(heng, hsem, hrb[sl][:], h_dram[hc, :, tq * 512:(tq + 1) * 512])
                        wait("G", hsem)
                        wait("G", "V", sbq_ready)
                        wait("G", "V", vch_done[sl])
                        op("G", lambda e, sl=sl: e.tensor_tensor(uq5[sl][:], hrb[sl][:], sBq[:], ALU.mult), tick="G")
                        pool_u[sl] = n["G"]
                        wait("V", "G", pool_u[sl])
                        wait("V", "P", prev_down_p)
                        op("V", lambda e, sl=sl: e.tensor_copy(k5i[:], uq5[sl][:]))
                        op("V", lambda e, sl=sl: e.tensor_copy(kf5[sl][:], k5i[:]))
                        op("V", lambda e, sl=sl: e.tensor_tensor(v5[sl][:], uq5[sl][:], kf5[sl][:], ALU.subtract))
                        op("V", lambda e, sl=sl: e.scalar_tensor_tensor(
                            t15[sl][:], v5[sl][:], 0.5, kf5[sl][:], ALU.is_gt, ALU.add))
                        op("V", lambda e, sl=sl, hc=hc: e.scalar_tensor_tensor(
                            hq[:, hc * 512:(hc + 1) * 512], v5[sl][:], -0.5, t15[sl][:],
                            ALU.is_lt, ALU.subtract), tick="V")
                        vch_done[sl] = n["V"]
                        vch_tick[hc] = n["V"]
                    if call == 1:
                        wait("T", "dS", terndn_dS)
                    else:
                        wait("T", "V", wd_all_V)
                    wait("T", "V", evac_all)
                    for K in range(NHC):
                        wait("T", "V", vch_tick[K])
                        for i in range(4):
                            lhs = hq[:, K * 512 + i * 128: K * 512 + (i + 1) * 128]
                            op("T", lambda e, lhs=lhs, K=K, i=i: e.matmul(
                                pb[2 * i][:], lhs, terndn[:, K * 1024: K * 1024 + 512],
                                start=(K == 0), stop=(K == NHC - 1)))
                            op("T", lambda e, lhs=lhs, K=K, i=i: e.matmul(
                                pb[2 * i + 1][:], lhs, terndn[:, K * 1024 + 512: (K + 1) * 1024],
                                start=(K == 0), stop=(K == NHC - 1)),
                               tick="P" if (K == NHC - 1 and i == 3) else None)
                    prev_down_p = n["P"]
                    wait("V", "P", prev_down_p)
                    for i in range(4):
                        tt = tq * 4 + i
                        for j in range(2):
                            b = 2 * i + j
                            xsl = x_sb[:, tt * 1024 + j * 512: tt * 1024 + (j + 1) * 512]
                            op("V", lambda e, b=b, xsl=xsl, tt=tt: e.scalar_tensor_tensor(
                                xsl, pb[b][:], st["cdown"][:, tt:tt + 1], xsl, ALU.mult, ALU.add),
                               tick="V")
                    evac_all = n["V"]
                    for b in range(8):
                        evac_cnt[b] = n["V"]

            def ln_stats(dest):
                for tt in range(NT):
                    xt = x_sb[:, tt * 1024:(tt + 1) * 1024]
                    op("A", lambda e, xt=xt, tt=tt: e.activation(
                        out=dest[:], in_=xt, func=AF.Identity, bias=0.0,
                        scale=1.0, accum_out=st["macc2"][:, tt:tt + 1]), drain=True)
                tick("A", "A")
                wait("V", "A")
                op("V", lambda e: e.tensor_single_scalar(st["mu_neg"][:], st["macc2"][:], -1.0 / D, ALU.mult), tick="V", drain=True)
                wait("A", "V")
                for tt in range(NT):
                    xt = x_sb[:, tt * 1024:(tt + 1) * 1024]
                    op("A", lambda e, xt=xt, tt=tt: e.activation(
                        out=dest[:], in_=xt, func=AF.Square,
                        bias=st["mu_neg"][:, tt:tt + 1], scale=1.0,
                        accum_out=st["vacc"][:, tt:tt + 1]), drain=True)
                tick("A", "A")
                wait("V", "A")
                op("V", lambda e: e.tensor_single_scalar(st["t16a"][:], st["vacc"][:], 1.0 / D, ALU.mult), tick="V", drain=True)
                wait("A", "V")
                op("A", lambda e: e.activation(out=st["t16b"][:], in_=st["t16a"][:],
                                               func=AF.Sqrt, bias=epsc[:], scale=1.0), tick="A", drain=True)
                wait("V", "A")
                op("V", lambda e: e.reciprocal(st["inv"][:], st["t16b"][:]), drain=True)
                op("V", lambda e: e.tensor_tensor(st["cB"][:], st["mu_neg"][:], st["inv"][:], ALU.mult), tick="V", drain=True)

            def phase_kv():
                ln_stats(xlb)
                stats_v = n["V"]
                kvsl3 = kvslab[:, :].rearrange("p (c o) -> p c o", o=2048)
                for h2 in range(2):
                    dma("S", "dS", kvsl3[:, h2 * 4:(h2 + 1) * 4, :],
                        wkvT4[:, h2 * 4:(h2 + 1) * 4, :])
                dma("S", "dS", bkv_sb[:], bkvB)
                slab_dS = n["dS"]
                kv_evac_last = 0
                kv_evac = {0: 0, 1: 0}
                cmm = {0: 0, 1: 0}
                xl_cons = 0
                xlT_ev = {0: 0, 1: 0}
                kvmm_p = {0: 0, 1: 0}
                for tt in range(NT):
                    sl = tt % 2
                    if tt == 0:
                        wait("A", "V", stats_v)
                    wait("A", "P", xl_cons)
                    op("A", lambda e, tt=tt: e.activation(
                        out=xlb[:], in_=x_sb[:, tt * 1024:(tt + 1) * 1024], func=AF.Identity,
                        bias=st["cB"][:, tt:tt + 1], scale=st["inv"][:, tt:tt + 1]), tick="A")
                    wait("T", "A")
                    for dc in range(ND):
                        tb = 4 + (dc % 2)
                        wait("T", "V", evac_cnt[tb])
                        op("T", lambda e, dc=dc, tb=tb: e.transpose(
                            pb[tb][:, 0:128], xlb[:, dc * 128:(dc + 1) * 128], idf[:]), tick="P")
                        if dc == ND - 1:
                            xl_cons = n["P"]
                        wait("V", "P")
                        wait("V", "P", kvmm_p[sl])
                        op("V", lambda e, dc=dc, sl=sl, tb=tb: e.tensor_copy(
                            xlT[sl][:, dc * 128:(dc + 1) * 128], pb[tb][:, 0:128]), tick="V")
                        evac_cnt[tb] = n["V"]
                    xlT_ev[sl] = n["V"]
                    if tt == 0:
                        wait("T", "dS", slab_dS)
                    wait("T", "V", xlT_ev[sl])
                    wait("T", "V", kv_evac_last)
                    for K in range(ND):
                        for b in range(4):
                            op("T", lambda e, K=K, b=b, sl=sl: e.matmul(
                                pb[b][:], xlT[sl][:, K * 128:(K + 1) * 128],
                                kvslab[:, K * 2048 + b * 512: K * 2048 + (b + 1) * 512],
                                start=(K == 0), stop=(K == ND - 1)),
                               tick="P" if (K == ND - 1 and b == 3) else None)
                    kvmm_p[sl] = n["P"]
                    wait("V", "P", kvmm_p[sl])
                    wait("V", "P", cmm[sl])
                    for b in range(4):
                        op("V", lambda e, b=b, sl=sl: e.tensor_tensor(
                            kvt[sl][:, b * 512:(b + 1) * 512], pb[b][:],
                            bkv_sb[:, b * 512:(b + 1) * 512], ALU.add), tick="V")
                    kv_evac[sl] = n["V"]
                    kv_evac_last = n["V"]
                    wait("T", "V", kv_evac_last)
                    for h in range(HEADS):
                        cb = 6 + h // 4
                        cs = (h % 4) * 128
                        op("T", lambda e, h=h, sl=sl, cb=cb, cs=cs, tt=tt: e.matmul(
                            pb[cb][:, cs:cs + 128],
                            kvt[sl][:, h * 128:(h + 1) * 128],
                            kvt[sl][:, 1024 + h * 128: 1024 + (h + 1) * 128],
                            start=(tt == 0 and h % 4 == 0),
                            stop=(tt == NT - 1 and h % 4 == 3)),
                           tick="P" if h == HEADS - 1 else None)
                    cmm[sl] = n["P"]
                wait("V", "P")
                op("V", lambda e: e.tensor_copy(cpart[:, 0:512], pb[6][:]))
                op("V", lambda e: e.tensor_copy(cpart[:, 512:1024], pb[7][:]), tick="V")
                evac_cnt[6] = n["V"]
                evac_cnt[7] = n["V"]
                wait("G", "V")
                dma("G", "dG", cc_in, cpart[:])
                wait("G", "dG")
                if role == "G":
                    bump("CC", 1, E.collective_compute(
                        "AllReduce", ALU.add,
                        replica_groups=[[0, 1, 2, 3], [4, 5, 6, 7]],
                        ins=[cc_in], outs=[cc_out]))
                else:
                    bump("CC", 1)
                wait("G", "CC")
                dma("G", "dG", csb[:], cc_out)

            def phase_q():
                wq3 = wq_sb[:, :].rearrange("p (c o) -> p c o", o=1024)
                wo3 = wo_sb[:, :].rearrange("p (c o) -> p c o", o=1024)
                dma("S", "dS", wq3, wqT4)
                dma("S", "dS", wo3, woT4)
                dma("S", "dS", bq_sb[:], bqB)
                dma("S", "dS", bo_sb[:], boB)
                wdma = n["dS"]
                wait("G", "dG")
                tick("G", "G")
                csb_g = n["G"]
                q_evac = {0: 0, 1: 0}
                q_evac_last = 0
                att_evac = 0
                out_evac = 0
                xl_cons = 0
                xlT_ev = {0: 0, 1: 0}
                qmm_p = {0: 0, 1: 0}
                qtrans_p = {0: 0, 1: 0}
                womm_p = {0: 0, 1: 0}
                qt_done = {0: 0, 1: 0}
                for tt in range(NT):
                    sl = tt % 2
                    wait("A", "P", xl_cons)
                    wait("A", "V", out_evac)
                    op("A", lambda e, tt=tt: e.activation(
                        out=xlb7[:], in_=x_sb[:, tt * 1024:(tt + 1) * 1024], func=AF.Identity,
                        bias=st["cB"][:, tt:tt + 1], scale=st["inv"][:, tt:tt + 1]), tick="A")
                    wait("T", "A")
                    for dc in range(ND):
                        tb = 2 + (dc % 2)
                        wait("T", "V", evac_cnt[tb])
                        op("T", lambda e, dc=dc, tb=tb: e.transpose(
                            pb[tb][:, 0:128], xlb7[:, dc * 128:(dc + 1) * 128], idf[:]), tick="P")
                        if dc == ND - 1:
                            xl_cons = n["P"]
                        wait("V", "P")
                        wait("V", "P", qmm_p[sl])
                        op("V", lambda e, dc=dc, sl=sl, tb=tb: e.tensor_copy(
                            xlT7[sl][:, dc * 128:(dc + 1) * 128], pb[tb][:, 0:128]), tick="V")
                        evac_cnt[tb] = n["V"]
                    xlT_ev[sl] = n["V"]
                    if tt == 0:
                        wait("T", "dS", wdma)
                    wait("T", "V", xlT_ev[sl])
                    wait("T", "V", q_evac_last)
                    for K in range(ND):
                        for b in range(2):
                            op("T", lambda e, K=K, b=b, sl=sl: e.matmul(
                                pb[b][:], xlT7[sl][:, K * 128:(K + 1) * 128],
                                wq_sb[:, K * 1024 + b * 512: K * 1024 + (b + 1) * 512],
                                start=(K == 0), stop=(K == ND - 1)),
                               tick="P" if (K == ND - 1 and b == 1) else None)
                    qmm_p[sl] = n["P"]
                    wait("V", "P", qmm_p[sl])
                    wait("V", "P", qtrans_p[sl])
                    for b in range(2):
                        op("V", lambda e, b=b, sl=sl: e.tensor_tensor(
                            qtb[sl][:, b * 512:(b + 1) * 512], pb[b][:],
                            bq_sb[:, b * 512:(b + 1) * 512], ALU.add), tick="V")
                    q_evac[sl] = n["V"]
                    q_evac_last = n["V"]
                    wait("T", "V", q_evac_last)
                    qT = [qT_sb, qT_sb2][sl]
                    for h in range(HEADS):
                        tb = 2 + (h % 2)
                        wait("T", "V", evac_cnt[tb])
                        op("T", lambda e, h=h, sl=sl, tb=tb: e.transpose(
                            pb[tb][:, 0:128], qtb[sl][:, h * 128:(h + 1) * 128], idf[:]), tick="P")
                        wait("V", "P")
                        wait("V", "P", qt_done[sl])
                        op("V", lambda e, h=h, tb=tb, qT=qT: e.tensor_copy(
                            qT[:, h * 128:(h + 1) * 128], pb[tb][:, 0:128]), tick="V")
                        evac_cnt[tb] = n["V"]
                    qtrans_p[sl] = n["P"]
                    qtev = n["V"]
                    if tt == 0:
                        wait("T", "G", csb_g)
                    wait("T", "V", qtev)
                    wait("T", "V", att_evac)
                    for h in range(HEADS):
                        cb = 4 + h // 4
                        cs = (h % 4) * 128
                        op("T", lambda e, h=h, cb=cb, cs=cs, qT=qT: e.matmul(
                            pb[cb][:, cs:cs + 128], csb[:, h * 128:(h + 1) * 128],
                            qT[:, h * 128:(h + 1) * 128],
                            start=(h % 4 == 0), stop=(h % 4 == 3)),
                           tick="P" if h == HEADS - 1 else None)
                    qt_done[sl] = n["P"]
                    wait("V", "P", qt_done[sl])
                    wait("V", "P", womm_p[sl])
                    op("V", lambda e, sl=sl: e.tensor_copy(attT[sl][:, 0:512], pb[4][:]))
                    op("V", lambda e, sl=sl: e.tensor_copy(attT[sl][:, 512:1024], pb[5][:]), tick="V")
                    att_evac = n["V"]
                    wait("T", "V", att_evac)
                    wait("T", "V", out_evac)
                    for K in range(ND):
                        for b in range(2):
                            op("T", lambda e, K=K, b=b, sl=sl: e.matmul(
                                pb[6 + b][:], attT[sl][:, K * 128:(K + 1) * 128],
                                wo_sb[:, K * 1024 + b * 512: K * 1024 + (b + 1) * 512],
                                start=(K == 0), stop=(K == ND - 1)),
                               tick="P" if (K == ND - 1 and b == 1) else None)
                    womm = n["P"]
                    womm_p[sl] = womm
                    wait("V", "P", womm)
                    op("V", lambda e, tt=tt: e.tensor_tensor(
                        x_sb[:, tt * 1024:(tt + 1) * 1024],
                        x_sb[:, tt * 1024:(tt + 1) * 1024], bo_sb[:], ALU.add))
                    for b in range(2):
                        xsl = x_sb[:, tt * 1024 + b * 512: tt * 1024 + (b + 1) * 512]
                        op("V", lambda e, b=b, xsl=xsl: e.tensor_tensor(
                            xsl, xsl, pb[6 + b][:], ALU.add), tick="V")
                    out_evac = n["V"]

            steps = [phase_quant_x, lambda: phase_gateup(0), phase_hscale,
                     lambda: phase_down(0), phase_kv, phase_q,
                     phase_quant_x, lambda: phase_gateup(1), phase_hscale,
                     lambda: phase_down(1)]
            for si, step in enumerate(steps):
                if si < phases:
                    step()
                    barrier()
            for q4 in range(4):
                dma("S", "dS", y3[:, q4 * 4:(q4 + 1) * 4, :],
                    x_sb3[:, q4 * 4:(q4 + 1) * 4, :])
            if dbg:
                dma("S", "dS", o_tg, tern_g[0])
                dma("S", "dS", o_h, h_dram[0])
                dma("S", "dS", o_xq, xqT[:, 0:2048])
                dma("S", "dS", o_mb[:, 0:4], mB[:])
                dma("S", "dS", o_mb[:, 4:8], mBh[:])
                dma("S", "dS", o_mb[:, 8:12], mBhn[:])
                dma("S", "dS", o_st[:, 0:16], st["sqh"][:])
                dma("S", "dS", o_st[:, 16:32], st["maxh"][:])
                dma("S", "dS", o_st[:, 32:48], st["qv"][:])
                dma("S", "dS", o_st[:, 48:64], st["cdown"][:])
                dma("S", "dS", o_sx, sxB[:])
                dma("S", "dS", o_qr, qrow[0:1, :])
            wait("S", "dS")
            barrier()

        @block.sync
        def _(e):
            script("S", e)

        @block.gpsimd
        def _(e):
            script("G", e)

        @block.tensor
        def _(e):
            script("T", e)

        @block.vector
        def _(e):
            script("V", e)

        @block.scalar
        def _(e):
            script("A", e)

    return nc


def _host_inputs(L0_attn_x, L0_attn_w_qkv, L0_attn_b_qkv, L0_attn_w_o, L0_attn_b_o,
                 L0_ffn_w_gate, L0_ffn_w_up, L0_ffn_w_down):
    x = np.asarray(L0_attn_x, dtype=np.float32)
    wqkv = np.asarray(L0_attn_w_qkv, dtype=np.float32)
    bqkv = np.asarray(L0_attn_b_qkv, dtype=np.float32)
    wo = np.asarray(L0_attn_w_o, dtype=np.float32)
    bo = np.asarray(L0_attn_b_o, dtype=np.float32)
    wg = np.asarray(L0_ffn_w_gate, dtype=np.float32)
    wu = np.asarray(L0_ffn_w_up, dtype=np.float32)
    wd = np.asarray(L0_ffn_w_down, dtype=np.float32)

    c_, h_, dd_ = np.meshgrid(np.arange(3), np.arange(8), np.arange(128), indexing="ij")
    perm = (c_ * 1024 + dd_ * 8 + h_).reshape(-1)
    wqkvT = np.ascontiguousarray(wqkv.T[:, perm])
    bperm = bqkv[perm]

    common = {
        "wkvT": np.ascontiguousarray(wqkvT[:, 1024:3072]),
        "wqT": np.ascontiguousarray(wqkvT[:, 0:1024]),
        "woT": np.ascontiguousarray(wo.T),
        "wgT": np.ascontiguousarray(
            wg.T.reshape(8, 128, 8, 512).transpose(2, 1, 0, 3).reshape(8, 128, 4096)),
        "wuT": np.ascontiguousarray(
            wu.T.reshape(8, 128, 8, 512).transpose(2, 1, 0, 3).reshape(8, 128, 4096)),
        "wdT": np.ascontiguousarray(wd.T),
        "bkvB": np.ascontiguousarray(np.broadcast_to(bperm[1024:3072], (128, 2048))),
        "bqB": np.ascontiguousarray(np.broadcast_to(bperm[0:1024], (128, 1024))),
        "boB": np.ascontiguousarray(np.broadcast_to(bo, (128, 1024))),
        "idf_in": np.eye(128, dtype=np.float32),
        "ones_in": np.ones((128, 128), dtype=np.float32),
        "eps_in": np.full((128, 1), 1e-5, dtype=np.float32),
    }
    in_maps = []
    for core in range(8):
        b, tc = core // 4, core % 4
        m = dict(common)
        m["x_in"] = np.ascontiguousarray(x[b, tc * TPC:(tc + 1) * TPC, :])
        in_maps.append(m)
    return in_maps


def kernel(**inputs):
    in_maps = _host_inputs(**inputs)
    nc = build_nc()
    res = run_bass_kernel_spmd(nc, in_maps, core_ids=list(range(8)))
    out = np.empty((2, 8192, 1024), dtype=np.float32)
    for core in range(8):
        b, tc = core // 4, core % 4
        out[b, tc * TPC:(tc + 1) * TPC, :] = res.results[core]["y_out"]
    return out
